# version 07
# baseline (speedup 1.0000x reference)
import contextlib

import numpy as np
import ml_dtypes

import concourse.bass as bass
import concourse.mybir as mybir
from concourse.bass_utils import run_bass_kernel_spmd

F32 = mybir.dt.float32
BF16 = mybir.dt.bfloat16
AF = mybir.ActivationFunctionType
ALU = mybir.AluOpType
AX = mybir.AxisListType

D = 2048
T = 2048
HT = 1024
NI = 8192
NH = 16
NKC = D // 128
EPS = 1e-5
NEGBIG = -1.0e30
QSC = 1.0 / float(np.sqrt(128.0))

ENGS = ("sync", "gpsimd", "tensor", "vector", "scalar")


class Plan:
    def __init__(self, nc, es):
        self.nc = nc
        self.ops = {e: [] for e in ENGS}
        self.sem = {}
        self.cnt = {}
        self.es = es

    def new_sem(self, name):
        self.sem[name] = self.es.enter_context(self.nc.semaphore(name))
        self.cnt[name] = 0

    def wait(self, eng, name, val=None):
        if val is None:
            val = self.cnt[name]
        if val <= 0:
            return
        h = self.sem[name]
        self.ops[eng].append(lambda e, h=h, v=val: e.wait_ge(h, v))

    def emit(self, eng, fn, sem=None, amt=1):
        if sem is None or amt is None:
            self.ops[eng].append(fn)
            return None
        h = self.sem[sem]
        self.cnt[sem] += amt
        self.ops[eng].append(lambda e, fn=fn, h=h, a=amt: fn(e).then_inc(h, a))
        return self.cnt[sem]

    def barrier(self, engs=ENGS):
        snap = dict(self.cnt)
        for eng in engs:
            for name, v in snap.items():
                self.wait(eng, name, v)


def build_program(upto="C", dumps=(), dbg_ffn=False):
    nc = bass.Bass("TRN2", target_bir_lowering=False, debug=False, num_devices=8)

    x0 = nc.dram_tensor("x0", [D, T], F32, kind="ExternalInput").ap()
    wqkv = nc.dram_tensor("wqkv", [D, 3 * D], F32, kind="ExternalInput").ap()
    wproj = nc.dram_tensor("wproj", [D, D], F32, kind="ExternalInput").ap()
    wg = nc.dram_tensor("wg", [D, NI], F32, kind="ExternalInput").ap()
    wu = nc.dram_tensor("wu", [D, NI], F32, kind="ExternalInput").ap()
    wd = nc.dram_tensor("wd", [NI, D], F32, kind="ExternalInput").ap()
    bqkv = nc.dram_tensor("bqkv", [128, 48], F32, kind="ExternalInput").ap()
    bproj = nc.dram_tensor("bproj", [128, 16], F32, kind="ExternalInput").ap()
    maskin = nc.dram_tensor("maskin", [128, 512], F32, kind="ExternalInput").ap()
    outx = nc.dram_tensor("outx", [D, T], F32, kind="ExternalOutput").ap()

    x1d = nc.dram_tensor("x1d", [D, T], F32, kind="Internal").ap()
    x2d = nc.dram_tensor("x2d", [D, T], F32, kind="Internal").ap()
    qstage = nc.dram_tensor("qstage", [D, T], BF16, kind="Internal").ap()
    kstage = nc.dram_tensor("kstage", [D, T], BF16, kind="Internal").ap()
    vstage = nc.dram_tensor("vstage", [T, D], BF16, kind="Internal").ap()
    kall = nc.dram_tensor("kall", [4 * D, T], BF16, kind="Internal").ap()
    vall = nc.dram_tensor("vall", [4 * T, D], BF16, kind="Internal").ap()
    aostage = nc.dram_tensor("aostage", [D, T], BF16, kind="Internal").ap()
    wg16 = nc.dram_tensor("wg16", [64, 128, 2048], BF16, kind="Internal").ap()
    wu16 = nc.dram_tensor("wu16", [64, 128, 2048], BF16, kind="Internal").ap()
    wd16 = nc.dram_tensor("wd16", [64, 128, 2048], BF16, kind="Internal").ap()

    dump_outs = {}
    for nm in dumps:
        src = {"x1": (x1d, F32, [D, T]), "x2": (x2d, F32, [D, T]),
               "kall": (kall, BF16, [4 * D, T]), "vall": (vall, BF16, [4 * T, D]),
               "q": (qstage, BF16, [D, T]), "ao": (aostage, BF16, [D, T]),
               "vstage": (vstage, BF16, [T, D]),
               "wg16": (wg16, BF16, [64, 128, 2048]),
               "wu16": (wu16, BF16, [64, 128, 2048]),
               "wd16": (wd16, BF16, [64, 128, 2048])}[nm]
        dump_outs[nm] = (nc.dram_tensor("dbg_" + nm, src[2], src[1],
                                        kind="ExternalOutput").ap(), src[0])

    if dbg_ffn:
        dbg_xn = nc.dram_tensor("dbg_xn", [128, NKC * HT], BF16, kind="ExternalOutput").ap()
        dbg_rows = nc.dram_tensor("dbg_rows", [2, HT], F32, kind="ExternalOutput").ap()
        dbg_acc = nc.dram_tensor("dbg_acc", [128, NKC * HT], F32, kind="ExternalOutput").ap()
    id_dram = nc.inline_tensor(np.eye(128, dtype=ml_dtypes.bfloat16), "id16d")

    _uid = [0]

    def _nm(n):
        _uid[0] += 1
        return "%s_%d" % (n, _uid[0])

    es = contextlib.ExitStack()
    with es:
        sbp = lambda n, s, d: es.enter_context(nc.sbuf_tensor(_nm(n), s, d))
        id16 = sbp("id16", [128, 128], BF16)
        bqkv_s = sbp("bqkv_s", [128, 48], F32)
        bqs_s = sbp("bqs_s", [128, 16], F32)
        bproj_s = sbp("bproj_s", [128, 16], F32)
        mask_s = sbp("mask_s", [128, 512], F32)
        ones_c = sbp("ones_c", [128, 1], F32)
        ones_r = sbp("ones_r", [1, 128], F32)
        r_a = sbp("r_a", [1, 1024], F32)
        r_b = sbp("r_b", [1, 1024], F32)
        rstd_row = sbp("rstd_row", [1, 1024], F32)
        nm_row = sbp("nm_row", [1, 1024], F32)

        P = Plan(nc, es)
        for s_ in ("qst", "qgst", "xl0", "xl1", "wa0", "wa1", "wb0", "wb1",
                   "wc0", "wc1", "kv0", "kv1", "cc", "t", "v", "s", "g",
                   "stq0", "stq1", "stv0", "stv1", "sta", "stx0", "stx1",
                   "stp0", "stp1"):
            P.new_sem(s_)

        P.emit("sync", lambda e: e.dma_start(out=id16[:], in_=id_dram.ap()), "xl0", 16)
        P.emit("sync", lambda e: e.dma_start(out=bqkv_s[:], in_=bqkv), "xl0", 16)
        P.emit("sync", lambda e: e.dma_start(out=bproj_s[:], in_=bproj), "xl0", 16)
        P.emit("sync", lambda e: e.dma_start(out=mask_s[:], in_=maskin), "xl0", 16)
        P.emit("gpsimd", lambda e: e.memset(ones_c[:], 1.0), "g", 1)
        P.emit("gpsimd", lambda e: e.memset(ones_r[:], 1.0), "g", 1)
        P.wait("vector", "xl0")
        P.emit("vector", lambda e: e.tensor_scalar(
            bqs_s[:], bqkv_s[:, 0:16], QSC, None, ALU.mult), "v", 1)
        P.barrier()


        def emit_stats_half(xsrc_fn, col0, xbufs, sqbufs, W=HT):
            with contextlib.ExitStack() as les:
                ps1 = les.enter_context(nc.psum_tensor(_nm("ps_s1"), [1, W], F32))
                ps2 = les.enter_context(nc.psum_tensor(_nm("ps_s2"), [1, W], F32))
                xb = [b[:, 0:W] for b in xbufs]
                sq = [b[:, 0:W] for b in sqbufs]
                P.wait("sync", "t")
                P.wait("sync", "s")
                P.wait("sync", "v")
                use_t = [0, 0]
                for k in range(NKC):
                    par = k % 2
                    if use_t[par]:
                        P.wait("sync", "t", use_t[par])
                    ld = P.emit("sync", lambda e, a=xsrc_fn(k), b=xb[par]:
                                e.dma_start(out=b, in_=a), "xl%d" % par, 16)
                    P.wait("scalar", "xl%d" % par, ld)
                    if use_t[par]:
                        P.wait("scalar", "t", use_t[par])
                    sc = P.emit("scalar", lambda e, i=xb[par], o=sq[par]:
                                e.activation(o, i, AF.Square), "s", 1)
                    P.wait("tensor", "xl%d" % par, ld)
                    for cb in range(W // 512):
                        P.emit("tensor", lambda e, r=xb[par], p=ps1, c=cb, k0=k:
                               e.matmul(p[:, c * 512:(c + 1) * 512], ones_c[:],
                                        r[:, c * 512:(c + 1) * 512],
                                        start=(k0 == 0), stop=(k0 == NKC - 1)),
                               "t", 1)
                    P.wait("tensor", "s", sc)
                    for cb in range(W // 512):
                        P.emit("tensor", lambda e, r=sq[par], p=ps2, c=cb, k0=k:
                               e.matmul(p[:, c * 512:(c + 1) * 512], ones_c[:],
                                        r[:, c * 512:(c + 1) * 512],
                                        start=(k0 == 0), stop=(k0 == NKC - 1)),
                               "t", 1)
                    use_t[par] = P.cnt["t"]
                P.wait("vector", "t")
                P.emit("vector", lambda e: e.tensor_scalar(
                    r_a[:, 0:W], ps1[:], 1.0 / D, None, ALU.mult), "v", 1)
                P.emit("vector", lambda e: e.tensor_scalar(
                    r_b[:, 0:W], ps2[:], 1.0 / D, None, ALU.mult), "v", 1)
                P.emit("vector", lambda e: e.tensor_tensor(
                    rstd_row[:, col0:col0 + W], r_a[:, 0:W], r_a[:, 0:W], ALU.mult), "v", 1)
                P.emit("vector", lambda e: e.tensor_tensor(
                    r_b[:, 0:W], r_b[:, 0:W], rstd_row[:, col0:col0 + W],
                    ALU.subtract), "v", 1)
                P.emit("vector", lambda e: e.tensor_scalar(
                    r_b[:, 0:W], r_b[:, 0:W], 1.0, EPS, ALU.mult, ALU.add), "v", 1)
                P.wait("scalar", "v")
                P.emit("scalar", lambda e: e.activation(
                    r_b[:, 0:W], r_b[:, 0:W], AF.Sqrt), "s", 1)
                P.wait("vector", "s")
                P.emit("vector", lambda e: e.reciprocal(
                    rstd_row[:, col0:col0 + W], r_b[:, 0:W]), "v", 1)
                P.emit("vector", lambda e: e.scalar_tensor_tensor(
                    nm_row[:, col0:col0 + W], r_a[:, 0:W], -1.0,
                    rstd_row[:, col0:col0 + W], ALU.mult, ALU.mult), "v", 1)
                P.wait("tensor", "v")

        def emit_bcast(col0, W, Rb, Mb):
            with contextlib.ExitStack() as les:
                nb = W // 512
                pbs = [les.enter_context(nc.psum_tensor(_nm("ps_b%d" % i), [128, 512], F32))
                       for i in range(min(nb, 4))]
                P.wait("tensor", "v")
                for i in range(nb):
                    P.emit("tensor", lambda e, i=i, p=pbs[i % len(pbs)]:
                           e.matmul(p[:], ones_r[:],
                                    rstd_row[0:1, col0 + i * 512: col0 + (i + 1) * 512],
                                    start=True, stop=True), "t", 1)
                P.wait("vector", "t")
                for i in range(nb):
                    P.emit("vector", lambda e, i=i, p=pbs[i % len(pbs)]:
                           e.tensor_copy(Rb[:, i * 512:(i + 1) * 512], p[:]), "v", 1)
                P.wait("tensor", "v")
                for i in range(nb):
                    P.emit("tensor", lambda e, i=i, p=pbs[i % len(pbs)]:
                           e.matmul(p[:], ones_r[:],
                                    nm_row[0:1, col0 + i * 512: col0 + (i + 1) * 512],
                                    start=True, stop=True), "t", 1)
                P.wait("vector", "t")
                for i in range(nb):
                    P.emit("vector", lambda e, i=i, p=pbs[i % len(pbs)]:
                           e.tensor_copy(Mb[:, i * 512:(i + 1) * 512], p[:]), "v", 1)
                P.wait("tensor", "v")

        def emit_apply(xsrc_fn, xn, Rb, Mb, W, tmpf, xb, xbsem):
            P.wait("sync", "t")
            P.wait("sync", "s")
            P.wait("sync", "v")
            use_v = [0, 0]
            for k in range(NKC):
                par = k % 2
                if use_v[par]:
                    P.wait("sync", "v", use_v[par])
                ld = P.emit("sync", lambda e, a=xsrc_fn(k), b=xb[par]:
                            e.dma_start(out=b[:], in_=a), xbsem[par], 16)
                P.wait("vector", xbsem[par], ld)
                P.emit("vector", lambda e, b=xb[par]: e.tensor_tensor(
                    tmpf[:, 0:W], b[:, 0:W], Rb[:, 0:W], ALU.mult), "v", 1)
                use_v[par] = P.emit("vector", lambda e, k0=k: e.tensor_tensor(
                    xn[:, k0 * W:(k0 + 1) * W], tmpf[:, 0:W], Mb[:, 0:W],
                    ALU.add), "v", 1)


        def emit_ffn(xin_chunk, xout_chunk, app):
            with contextlib.ExitStack() as fes:
                fsb = lambda n, s, d: fes.enter_context(nc.sbuf_tensor(_nm(n), s, d))
                xn = fsb("ffn_xn", [128, NKC * HT], BF16)
                acc = fsb("ffn_acc", [128, NKC * HT], F32)
                slab = fsb("ffn_slab", [128, 8 * HT], BF16)
                Rb = fsb("ffn_Rb", [128, HT], F32)
                Mb = fsb("ffn_Mb", [128, HT], F32)
                tmpf = fsb("ffn_tmpf", [128, HT], F32)
                xb = [fsb("ffn_xb0", [128, HT], F32), fsb("ffn_xb1", [128, HT], F32)]
                wgb = [fsb("wgb0", [128, 2048], BF16), fsb("wgb1", [128, 2048], BF16)]
                wub = [fsb("wub0", [128, 2048], BF16), fsb("wub1", [128, 2048], BF16)]
                wdb = [fsb("wdb0", [128, 2048], BF16), fsb("wdb1", [128, 2048], BF16)]
                hq = [fsb("hq%d" % r, [128, HT], BF16) for r in range(4)]
                na = [fsb("na%d" % r, [128, HT], BF16) for r in range(4)]
                tA = fsb("tA", [128, HT], BF16)
                tC = fsb("tC", [128, HT], BF16)
                tD = fsb("tD", [128, HT], BF16)
                x1b = [fsb("x1b0", [128, HT], F32), fsb("x1b1", [128, HT], F32)]

                wg_use = [0, 0]
                wd_use = [0, 0]

                for half in range(2):
                    fresh = (app == "A" and half == 0)

                    def wsrc(which, ic):
                        if fresh:
                            if which == "d":
                                g0, dp = ic // 8, ic % 8
                                return wd[g0 * 1024:(g0 + 1) * 1024,
                                          dp * 256:(dp + 1) * 256].rearrange(
                                              "(s p) d -> p s d", p=128)
                            src = wg if which == "g" else wu
                            return src[:, ic * 128:(ic + 1) * 128].rearrange(
                                "(k p) i -> p k i", p=128)
                        cache = {"g": wg16, "u": wu16, "d": wd16}[which]
                        return cache[ic]

                    emit_stats_half(lambda k: xin_chunk(k, half), 0, xb, x1b, HT)
                    if dbg_ffn and app == "A" and half == 0:
                        P.wait("sync", "v")
                        P.emit("sync", lambda e: e.dma_start(
                            out=dbg_rows[0:1, :], in_=rstd_row[:]), "qst", 16)
                        P.emit("sync", lambda e: e.dma_start(
                            out=dbg_rows[1:2, :], in_=nm_row[:]), "qst", 16)
                    emit_bcast(0, HT, Rb, Mb)
                    emit_apply(lambda k: xin_chunk(k, half), xn, Rb, Mb, HT,
                               tmpf, xb, ("xl0", "xl1"))
                    xn_done_v = P.cnt["v"]
                    if dbg_ffn and app == "A" and half == 0:
                        P.wait("sync", "v", xn_done_v)
                        P.emit("sync", lambda e: e.dma_start(
                            out=dbg_xn, in_=xn[:]), "qst", 16)

                    with contextlib.ExitStack() as pes:
                        ps_g = [pes.enter_context(nc.psum_tensor(_nm("ps_g%d" % i), [128, 512], F32))
                                for i in range(2)]
                        ps_u = [pes.enter_context(nc.psum_tensor(_nm("ps_u%d" % i), [128, 512], F32))
                                for i in range(2)]
                        ps_d = [pes.enter_context(nc.psum_tensor(_nm("ps_d%d" % i), [128, 512], F32))
                                for i in range(4)]
                        P.wait("tensor", "v", xn_done_v)
                        gate_rel = {0: 0, 1: 0}
                        up_rel = {0: 0, 1: 0}
                        dn_rel = {0: 0, 1: 0, 2: 0, 3: 0}
                        slab_free_t = 0
                        hq_free_v = 0

                        for g0 in range(8):
                            for ccq in range(2):
                                cc = g0 * 2 + ccq
                                for r in range(4):
                                    ic = 4 * cc + r
                                    wpar = (cc * 4 + r) % 2
                                    sg, su = "wa%d" % wpar, "wb%d" % wpar
                                    gq = "gpsimd" if fresh else "sync"
                                    uq = "gpsimd" if fresh else "scalar"
                                    if wg_use[wpar]:
                                        P.wait(gq, "t", wg_use[wpar])
                                        P.wait(uq, "t", wg_use[wpar])
                                    if fresh:
                                        P.wait("gpsimd", "qgst")
                                    dstg = wgb[wpar].ap().rearrange("p (k i) -> p k i", k=16) \
                                        if fresh else wgb[wpar][:]
                                    lg = P.emit(gq, lambda e, a=wsrc("g", ic), b=dstg:
                                                e.dma_start(out=b, in_=a), sg, 16)
                                    if fresh:
                                        P.wait("scalar", sg, lg)
                                        P.emit("scalar", lambda e, b=wgb[wpar], o=wg16[ic]:
                                               e.dma_start(out=o, in_=b[:]), "qgst", 16)
                                    dstu = wub[wpar].ap().rearrange("p (k i) -> p k i", k=16) \
                                        if fresh else wub[wpar][:]
                                    lu = P.emit(uq, lambda e, a=wsrc("u", ic), b=dstu:
                                                e.dma_start(out=b, in_=a), su, 16)
                                    if fresh:
                                        P.wait("scalar", su, lu)
                                        P.emit("scalar", lambda e, b=wub[wpar], o=wu16[ic]:
                                               e.dma_start(out=o, in_=b[:]), "qgst", 16)
                                    P.wait("tensor", sg, lg)
                                    P.wait("tensor", su, lu)
                                    for tb in range(2):
                                        if gate_rel[tb]:
                                            P.wait("tensor", "s", gate_rel[tb])
                                        for k in range(NKC):
                                            P.emit("tensor",
                                                   lambda e, k0=k, pg=ps_g[tb], b=wgb[wpar], tb0=tb:
                                                   e.matmul(pg[:], b[:, k0 * 128:(k0 + 1) * 128],
                                                            xn[:, k0 * HT + tb0 * 512: k0 * HT + tb0 * 512 + 512],
                                                            start=(k0 == 0), stop=(k0 == NKC - 1)),
                                                   "t", 1 if k == NKC - 1 else None)
                                        if up_rel[tb]:
                                            P.wait("tensor", "v", up_rel[tb])
                                        for k in range(NKC):
                                            P.emit("tensor",
                                                   lambda e, k0=k, pu=ps_u[tb], b=wub[wpar], tb0=tb:
                                                   e.matmul(pu[:], b[:, k0 * 128:(k0 + 1) * 128],
                                                            xn[:, k0 * HT + tb0 * 512: k0 * HT + tb0 * 512 + 512],
                                                            start=(k0 == 0), stop=(k0 == NKC - 1)),
                                                   "t", 1 if k == NKC - 1 else None)
                                    wg_use[wpar] = P.cnt["t"]
                                    tcnt = P.cnt["t"]
                                    gs = {0: tcnt - 3, 1: tcnt - 1}
                                    us = {0: tcnt - 2, 1: tcnt}
                                    for tb in range(2):
                                        P.wait("scalar", "t", gs[tb])
                                        if hq_free_v:
                                            P.wait("scalar", "v", hq_free_v)
                                        sc = P.emit("scalar", lambda e, pg=ps_g[tb], r0=r, tb0=tb:
                                                    e.activation(hq[r0][:, tb0 * 512: tb0 * 512 + 512],
                                                                 pg[:], AF.Silu), "s", 1)
                                        gate_rel[tb] = sc
                                        P.wait("vector", "s", sc)
                                        P.wait("vector", "t", us[tb])
                                        mv = P.emit("vector", lambda e, pu=ps_u[tb], r0=r, tb0=tb:
                                                    e.tensor_tensor(
                                                        hq[r0][:, tb0 * 512: tb0 * 512 + 512],
                                                        hq[r0][:, tb0 * 512: tb0 * 512 + 512],
                                                        pu[:], ALU.mult), "v", 1)
                                        up_rel[tb] = mv
                                if slab_free_t:
                                    P.wait("vector", "t", slab_free_t)
                                for r in range(4):
                                    P.emit("vector", lambda e, r0=r: e.scalar_tensor_tensor(
                                        na[r0][:], hq[r0][:], -1.0, hq[r0][:],
                                        ALU.mult, ALU.min))
                                P.emit("vector", lambda e: e.tensor_tensor(tA[:], na[0][:], na[1][:], ALU.min))
                                P.emit("vector", lambda e: e.tensor_tensor(tC[:], na[2][:], na[3][:], ALU.min))
                                P.emit("vector", lambda e: e.tensor_tensor(tA[:], tA[:], tC[:], ALU.max))
                                P.emit("vector", lambda e: e.tensor_tensor(tC[:], na[0][:], na[1][:], ALU.max))
                                P.emit("vector", lambda e: e.tensor_tensor(tD[:], na[2][:], na[3][:], ALU.max))
                                P.emit("vector", lambda e: e.tensor_tensor(tC[:], tC[:], tD[:], ALU.min))
                                P.emit("vector", lambda e: e.tensor_tensor(tA[:], tA[:], tC[:], ALU.min))
                                for r in range(4):
                                    P.emit("vector", lambda e, r0=r: e.tensor_tensor(
                                        tD[:], na[r0][:], tA[:], ALU.is_le))
                                    slot = (cc - 2 * g0) * 4 + r
                                    hq_free_v = P.emit("vector", lambda e, r0=r, s0=slot:
                                                       e.tensor_tensor(
                                                           slab[:, s0 * HT:(s0 + 1) * HT],
                                                           hq[r0][:], tD[:], ALU.mult), "v", 1)
                            net_done_v = P.cnt["v"]
                            P.wait("tensor", "v", net_done_v)
                            for dp in range(8):
                                wpar = dp % 2
                                sd = "wc%d" % wpar
                                dq = "gpsimd" if fresh else ("sync" if wpar == 0 else "scalar")
                                if wd_use[wpar]:
                                    P.wait(dq, "t", wd_use[wpar])
                                if fresh:
                                    P.wait("gpsimd", "qgst")
                                dstd = wdb[wpar].ap().rearrange("p (s d) -> p s d", s=8) \
                                    if fresh else wdb[wpar][:]
                                ldd = P.emit(dq, lambda e, a=wsrc("d", g0 * 8 + dp), b=dstd:
                                             e.dma_start(out=b, in_=a), sd, 16)
                                if fresh:
                                    P.wait("scalar", sd, ldd)
                                    P.emit("scalar", lambda e, b=wdb[wpar], o=wd16[g0 * 8 + dp]:
                                           e.dma_start(out=o, in_=b[:]), "qgst", 16)
                                P.wait("tensor", sd, ldd)
                                for dd in range(2):
                                    d0 = 2 * dp + dd
                                    for tb in range(2):
                                        dpi = dd * 2 + tb
                                        if dn_rel[dpi]:
                                            P.wait("tensor", "v", dn_rel[dpi])
                                        for s0 in range(8):
                                            P.emit("tensor",
                                                   lambda e, s1=s0, pd=ps_d[dpi], b=wdb[wpar],
                                                   tb0=tb, dd0=dd:
                                                   e.matmul(pd[:], b[:, s1 * 256 + dd0 * 128: s1 * 256 + dd0 * 128 + 128],
                                                            slab[:, s1 * HT + tb0 * 512: s1 * HT + tb0 * 512 + 512],
                                                            start=(s1 == 0), stop=(s1 == 7)),
                                                   "t", 1 if s0 == 7 else None)
                                        mmc = P.cnt["t"]
                                        P.wait("vector", "t", mmc)
                                        if g0 == 0:
                                            dn_rel[dpi] = P.emit(
                                                "vector", lambda e, pd=ps_d[dpi], d1=d0, tb0=tb:
                                                e.tensor_copy(
                                                    acc[:, d1 * HT + tb0 * 512: d1 * HT + tb0 * 512 + 512],
                                                    pd[:]), "v", 1)
                                        else:
                                            dn_rel[dpi] = P.emit(
                                                "vector", lambda e, pd=ps_d[dpi], d1=d0, tb0=tb:
                                                e.tensor_tensor(
                                                    acc[:, d1 * HT + tb0 * 512: d1 * HT + tb0 * 512 + 512],
                                                    acc[:, d1 * HT + tb0 * 512: d1 * HT + tb0 * 512 + 512],
                                                    pd[:], ALU.add), "v", 1)
                                wd_use[wpar] = P.cnt["t"]
                            slab_free_t = P.cnt["t"]

                    if dbg_ffn and app == "A" and half == 0:
                        P.wait("sync", "v")
                        P.emit("sync", lambda e: e.dma_start(
                            out=dbg_acc, in_=acc[:]), "qst", 16)
                        P.wait("sync", "qst")
                    P.wait("sync", "t")
                    P.wait("sync", "s")
                    P.wait("sync", "v")
                    usef = [0, 0]
                    stcnt = [0, 0]
                    for k in range(NKC):
                        par = k % 2
                        if usef[par]:
                            P.wait("sync", "v", usef[par])
                        ldf = P.emit("sync", lambda e, a=xin_chunk(k, half), b=xb[par]:
                                     e.dma_start(out=b[:], in_=a), "xl%d" % par, 16)
                        P.wait("vector", "xl%d" % par, ldf)
                        if stcnt[par]:
                            P.wait("vector", "stx%d" % par, stcnt[par])
                        usef[par] = P.emit("vector", lambda e, k0=k, b=xb[par], o=x1b[par]:
                                           e.scalar_tensor_tensor(
                                               o[:], acc[:, k0 * HT:(k0 + 1) * HT], 0.5,
                                               b[:], ALU.mult, ALU.add), "v", 1)
                        P.wait("sync", "v", usef[par])
                        stcnt[par] = P.emit("sync", lambda e, o=x1b[par], dst=xout_chunk(k, half):
                                            e.dma_start(out=dst, in_=o[:]), "stx%d" % par, 16)
                    P.barrier()


        if upto in ("A", "A2", "B", "C"):
            emit_ffn(lambda k, h: x0[k * 128:(k + 1) * 128, h * HT:(h + 1) * HT],
                     lambda k, h: x1d[k * 128:(k + 1) * 128, h * HT:(h + 1) * HT],
                     "A")

        if upto in ("A2", "B", "C"):
            with contextlib.ExitStack() as aes:
                asb = lambda n, s, d: aes.enter_context(nc.sbuf_tensor(_nm(n), s, d))
                Rb2 = asb("a2_Rb", [128, T], F32)
                Mb2 = asb("a2_Mb", [128, T], F32)
                tmp2 = asb("a2_tmp", [128, T], F32)
                xb2 = [asb("a2_xb0", [128, T], F32), asb("a2_xb1", [128, T], F32)]
                a2sq = asb("a2_sq", [128, HT], F32)
                xn2 = asb("a2_xn", [128, NKC * T], BF16)
                for half in range(2):
                    emit_stats_half(
                        lambda k, h=half: x1d[k * 128:(k + 1) * 128, h * HT:(h + 1) * HT],
                        0, xb2, [tmp2, a2sq], HT)
                    emit_bcast(0, HT, Rb2[:, half * HT:(half + 1) * HT],
                               Mb2[:, half * HT:(half + 1) * HT])
                emit_apply(lambda k: x1d[k * 128:(k + 1) * 128, :], xn2, Rb2, Mb2, T,
                           tmp2, xb2, ("xl0", "xl1"))
                xn2_done = P.cnt["v"]
                qsg = [asb("a2_qs0", [128, T], BF16), asb("a2_qs1", [128, T], BF16)]
                vtb = [asb("a2_vt0", [128, 1024], BF16), asb("a2_vt1", [128, 1024], BF16)]
                wqb = [asb("a2_wq0", [128, 2048], BF16), asb("a2_wq1", [128, 2048], BF16)]
                with contextlib.ExitStack() as pes:
                    ps_q = [pes.enter_context(nc.psum_tensor(_nm("ps_q%d" % i), [128, 1024], F32))
                            for i in range(3)]
                    ps_vt = [pes.enter_context(nc.psum_tensor(_nm("ps_vt%d" % i), [128, 1024], BF16))
                             for i in range(2)]
                    P.wait("tensor", "v", xn2_done)
                    wq_use = [0, 0]
                    q_rel = {0: 0, 1: 0, 2: 0}
                    qslot = [0]
                    stg_rel = [0, 0]
                    stg_rel_t = [0, 0]
                    vt_rel = {0: 0, 1: 0}
                    vtb_rel = [0, 0]
                    forder = list(range(16, 48)) + list(range(16))
                    for fi, f in enumerate(forder):
                        wpar = fi % 2
                        sg = "wa%d" % wpar
                        if wq_use[wpar]:
                            P.wait("gpsimd", "t", wq_use[wpar])
                        lq = P.emit("gpsimd", lambda e,
                                    b=wqb[wpar].ap().rearrange("p (k i) -> p k i", k=16),
                                    a=wqkv[:, f * 128:(f + 1) * 128].rearrange(
                                        "(k p) i -> p k i", p=128):
                                    e.dma_start(out=b, in_=a), sg, 16)
                        P.wait("tensor", sg, lq)
                        slots = []
                        for hb in range(2):
                            sl = qslot[0]
                            qslot[0] = (qslot[0] + 1) % 3
                            slots.append(sl)
                            if q_rel[sl]:
                                P.wait("tensor", "v", q_rel[sl])
                            for cb in range(2):
                                for k in range(NKC):
                                    P.emit("tensor",
                                           lambda e, k0=k, hb0=hb, cb0=cb, b=wqb[wpar], sl0=sl:
                                           e.matmul(ps_q[sl0][:, cb0 * 512:(cb0 + 1) * 512],
                                                    b[:, k0 * 128:(k0 + 1) * 128],
                                                    xn2[:, k0 * T + hb0 * 1024 + cb0 * 512:
                                                        k0 * T + hb0 * 1024 + cb0 * 512 + 512],
                                                    start=(k0 == 0), stop=(k0 == NKC - 1)),
                                           "t", 1 if (k == NKC - 1 and cb == 1) else None)
                        wq_use[wpar] = P.cnt["t"]
                        tq = P.cnt["t"]
                        spar = fi % 2
                        if stg_rel[spar]:
                            P.wait("vector", "stq%d" % spar, stg_rel[spar])
                        if stg_rel_t[spar]:
                            P.wait("vector", "t", stg_rel_t[spar])
                        bias_ap = bqkv_s[:, f:f + 1]
                        scl = QSC if f < 16 else None
                        for hb in range(2):
                            sl = slots[hb]
                            P.wait("vector", "t", tq - 1 + hb)
                            q_rel[sl] = P.emit("vector",
                                               lambda e, hb0=hb, b=bias_ap, sc0=scl, sp=spar, sl0=sl:
                                               e.tensor_scalar(qsg[sp][:, hb0 * 1024:(hb0 + 1) * 1024],
                                                               ps_q[sl0][:], b, sc0, ALU.add,
                                                               ALU.mult) if sc0 is not None else
                                               e.tensor_scalar(qsg[sp][:, hb0 * 1024:(hb0 + 1) * 1024],
                                                               ps_q[sl0][:], b, None, ALU.add),
                                               "v", 1)
                        se = P.cnt["v"]
                        if f < 32:
                            dst = (qstage if f < 16 else kstage)
                            row = (f % 16) * 128
                            P.wait("sync", "v", se)
                            stg_rel[spar] = P.emit("sync", lambda e, sp=spar, d=dst, r0=row:
                                                   e.dma_start(out=d[r0:r0 + 128, :],
                                                               in_=qsg[sp][:]),
                                                   "stq%d" % spar, 16)
                        else:
                            h0 = f - 32
                            P.wait("tensor", "v", se)
                            for w0 in range(2):
                                if vt_rel[w0]:
                                    P.wait("tensor", vt_rel[w0][0], vt_rel[w0][1])
                                for tc in range(8):
                                    tcg = w0 * 8 + tc
                                    P.emit("tensor", lambda e, sp=spar, t0=tcg, w1=w0, tc0=tc:
                                           e.transpose(ps_vt[w1][:, tc0 * 128:(tc0 + 1) * 128],
                                                       qsg[sp][:, t0 * 128:(t0 + 1) * 128],
                                                       id16[:]),
                                           "t", 1 if tc == 7 else None)
                                tw = P.cnt["t"]
                                P.wait("scalar", "t", tw)
                                if vtb_rel[w0]:
                                    P.wait("scalar", "stv%d" % w0, vtb_rel[w0])
                                vt_rel_s = P.emit("scalar", lambda e, w1=w0:
                                                  e.activation(vtb[w1][:], ps_vt[w1][:],
                                                               AF.Copy), "s", 1)
                                vt_rel[w0] = ("s", vt_rel_s)
                                P.wait("sync", "s", vt_rel_s)
                                dstv = vstage[w0 * 1024: w0 * 1024 + 1024,
                                              h0 * 128:(h0 + 1) * 128].rearrange(
                                                  "(tc p) dh -> p tc dh", p=128)
                                srcv = vtb[w0].ap().rearrange("p (tc dh) -> p tc dh", tc=8)
                                vtb_rel[w0] = P.emit("sync", lambda e, dv=dstv, sv=srcv:
                                                     e.dma_start(out=dv, in_=sv),
                                                     "stv%d" % w0, 16)
                            stg_rel[spar] = P.cnt["stq%d" % spar]
                            stg_rel_t[spar] = P.cnt["t"]
                        if f == 31:
                            P.wait("gpsimd", "stq0")
                            P.wait("gpsimd", "stq1")
                            P.emit("gpsimd", lambda e: e.collective_compute(
                                "AllGather", ALU.bypass,
                                replica_groups=[[0, 1, 2, 3], [4, 5, 6, 7]],
                                ins=[kstage], outs=[kall]), "cc", 1)
                        if f == 47:
                            P.wait("gpsimd", "stv0")
                            P.wait("gpsimd", "stv1")
                            P.emit("gpsimd", lambda e: e.collective_compute(
                                "AllGather", ALU.bypass,
                                replica_groups=[[0, 1, 2, 3], [4, 5, 6, 7]],
                                ins=[vstage], outs=[vall]), "cc", 1)
                    P.barrier()

        if upto in ("B", "C"):
            with contextlib.ExitStack() as bes:
                bsb = lambda n, s, d: bes.enter_context(nc.sbuf_tensor(_nm(n), s, d))
                Kb = [bsb("b_K0", [128, 8192], BF16), bsb("b_K1", [128, 8192], BF16)]
                Vb = [bsb("b_V0", [128, 8192], BF16), bsb("b_V1", [128, 8192], BF16)]
                Qb = [bsb("b_Q0", [128, T], BF16), bsb("b_Q1", [128, T], BF16)]
                Ssb = bsb("b_S", [128, 8192], F32)
                Pb = [bsb("b_P0", [128, 8192], BF16), bsb("b_P1", [128, 8192], BF16)]
                PTb = [bsb("b_PT0", [128, 1024], BF16), bsb("b_PT1", [128, 1024], BF16)]
                mxb = bsb("b_mx", [128, 16], F32)
                negm = [bsb("b_negm0", [128, 1], F32), bsb("b_negm1", [128, 1], F32)]
                lrow = [bsb("b_l0", [128, 1], F32), bsb("b_l1", [128, 1], F32)]
                linv = [bsb("b_linv0", [128, 1], F32), bsb("b_linv1", [128, 1], F32)]
                obuf = bsb("b_o", [128, 128], BF16)
                aob = bsb("b_ao", [128, T], BF16)

                kall_r = kall.rearrange("(c4 hh p) (c16 w) -> hh p c16 c4 w",
                                        c4=4, hh=16, p=128, c16=16)
                vall_r = vall.rearrange("(c4 c16 p) (hh w) -> hh p c16 c4 w",
                                        c4=4, c16=16, p=128, hh=16)

                with contextlib.ExitStack() as pes:
                    ps_s = [pes.enter_context(nc.psum_tensor(_nm("ps_sS%d" % i), [128, 1024], F32))
                            for i in range(2)]
                    ps_pt = [pes.enter_context(nc.psum_tensor(_nm("ps_pt%d" % i), [128, 1024], BF16))
                             for i in range(2)]
                    ps_o = pes.enter_context(nc.psum_tensor(_nm("ps_o"), [128, 128], F32))
                    ps_ao = pes.enter_context(nc.psum_tensor(_nm("ps_ao"), [128, 1024], BF16))

                    P.wait("gpsimd", "cc", 2)
                    P.wait("gpsimd", "stq0")
                    P.wait("gpsimd", "stq1")
                    head_rel = [0, 0]
                    srel = {0: [], 1: []}
                    pt_rel = {0: None, 1: None}
                    o_rel_c = [0]
                    ao_evac_c = [0]
                    recip_c = [0, 0]
                    aob_rel = 0
                    for h in range(NH):
                        par = h % 2
                        if head_rel[par]:
                            P.wait("gpsimd", "t", head_rel[par])
                        kvsem = "kv%d" % par
                        for c4 in range(4):
                            P.emit("gpsimd", lambda e, src=kall_r[h, :, :, c4, :],
                                   b=Kb[par], c40=c4: e.dma_start(
                                       out=b.ap().rearrange("p (c16 c4 w) -> p c16 c4 w",
                                                            c16=16, c4=4, w=128)[:, :, c40, :],
                                       in_=src), kvsem, 16)
                        for c4 in range(4):
                            P.emit("gpsimd", lambda e, src=vall_r[h, :, :, c4, :],
                                   b=Vb[par], c40=c4: e.dma_start(
                                       out=b.ap().rearrange("p (c16 c4 w) -> p c16 c4 w",
                                                            c16=16, c4=4, w=128)[:, :, c40, :],
                                       in_=src), kvsem, 16)
                        lq = P.emit("gpsimd", lambda e, b=Qb[par], h0=h:
                                    e.dma_start(out=b[:],
                                                in_=qstage[h0 * 128:(h0 + 1) * 128, :]),
                                    kvsem, 16)
                        P.wait("tensor", kvsem, lq)

                        state = {}

                        def emit_sphase(t0):
                            tp = t0 % 2
                            nb = t0 + 1
                            npair = (nb + 1) // 2
                            for j in range(npair):
                                sp = j % 2
                                blocks = [2 * j] + ([2 * j + 1] if 2 * j + 1 < nb else [])
                                for en, vv in srel[sp]:
                                    P.wait("tensor", en, vv)
                                for bi, b0 in enumerate(blocks):
                                    P.emit("tensor",
                                           lambda e, b1=b0, c0=bi * 512, t1=t0, b=Kb[par],
                                           q=Qb[par], p=ps_s[sp]:
                                           e.matmul(p[:, c0:c0 + 512],
                                                    q[:, t1 * 128:(t1 + 1) * 128],
                                                    b[:, b1 * 512:(b1 + 1) * 512],
                                                    start=True, stop=True),
                                           "t", 1 if bi == len(blocks) - 1 else None)
                                mm_t = P.cnt["t"]
                                rel = []
                                plain = [b for b in blocks if b != t0]
                                if t0 in blocks:
                                    ci = (t0 - 2 * j) * 512
                                    P.wait("vector", "t", mm_t)
                                    ev = P.emit("vector", lambda e, ci0=ci, p=ps_s[sp], t1=t0:
                                                e.scalar_tensor_tensor(
                                                    Ssb[:, t1 * 512:(t1 + 1) * 512],
                                                    p[:, ci0:ci0 + 512], 1.0, mask_s[:],
                                                    ALU.mult, ALU.add), "v", 1)
                                    rel.append(("v", ev))
                                if plain:
                                    c0 = (plain[0] - 2 * j) * 512
                                    w_ = 512 * len(plain)
                                    if j % 2 == 0:
                                        P.wait("scalar", "t", mm_t)
                                        ev = P.emit("scalar",
                                                    lambda e, c1=c0, w1=w_, p=ps_s[sp], bb=plain[0]:
                                                    e.activation(Ssb[:, bb * 512: bb * 512 + w1],
                                                                 p[:, c1:c1 + w1], AF.Copy),
                                                    "s", 1)
                                        rel.append(("s", ev))
                                    else:
                                        P.wait("vector", "t", mm_t)
                                        ev = P.emit("vector",
                                                    lambda e, c1=c0, w1=w_, p=ps_s[sp], bb=plain[0]:
                                                    e.tensor_copy(Ssb[:, bb * 512: bb * 512 + w1],
                                                                  p[:, c1:c1 + w1]),
                                                    "v", 1)
                                        rel.append(("v", ev))
                                srel[sp] = rel
                                for en, vv in rel:
                                    if en != "v":
                                        P.wait("vector", en, vv)
                                wpr = 512 * len(blocks)
                                P.emit("vector", lambda e, j0=j, w1=wpr: e.reduce_max(
                                    mxb[:, j0:j0 + 1], Ssb[:, j0 * 1024: j0 * 1024 + w1],
                                    axis=AX.X))
                            nm_v = P.emit("vector", lambda e, n=npair, tp0=tp: e.reduce_max(
                                negm[tp0][:], mxb[:, 0:n], axis=AX.X, negate=True), "v", 1)
                            P.wait("scalar", "v", nm_v)
                            if state.get("recip%d" % tp):
                                P.wait("scalar", "v", state["recip%d" % tp])
                            if state.get("ptdone%d" % tp):
                                P.wait("scalar", "t", state["ptdone%d" % tp])
                            exp_s = P.emit("scalar", lambda e, kl=512 * nb, tp0=tp:
                                           e.activation(Pb[tp0][:, 0:kl], Ssb[:, 0:kl],
                                                        AF.Exp, bias=negm[tp0][:], scale=1.0,
                                                        accum_out=lrow[tp0][:]), "s", 1)
                            P.wait("vector", "s", exp_s)
                            state["recip%d" % tp] = P.emit(
                                "vector", lambda e, tp0=tp: e.reciprocal(
                                    linv[tp0][:], lrow[tp0][:]), "v", 1)
                            state["exp%d" % tp] = exp_s

                        def emit_ptpv(t0):
                            tp = t0 % 2
                            nkchunks = 4 * (t0 + 1)
                            P.wait("tensor", "s", state["exp%d" % tp])
                            nw = (nkchunks + 7) // 8
                            ptv_of = {}

                            def emit_T(w0):
                                wpar2 = w0 % 2
                                cn = min(8, nkchunks - w0 * 8)
                                if pt_rel[wpar2] is not None:
                                    P.wait("tensor", pt_rel[wpar2][0], pt_rel[wpar2][1])
                                for i in range(cn):
                                    kc = w0 * 8 + i
                                    P.emit("tensor", lambda e, kc0=kc, i0=i, wp=wpar2, tp0=tp:
                                           e.transpose(ps_pt[wp][:, i0 * 128:(i0 + 1) * 128],
                                                       Pb[tp0][:, kc0 * 128:(kc0 + 1) * 128],
                                                       id16[:]),
                                           "t", 1 if i == cn - 1 else None)
                                tw = P.cnt["t"]
                                if w0 % 2 == 0:
                                    P.wait("vector", "t", tw)
                                    ptv = P.emit("vector", lambda e, wp=wpar2, c=cn:
                                                 e.tensor_copy(PTb[wp][:, 0:c * 128],
                                                               ps_pt[wp][:, 0:c * 128]), "v", 1)
                                    pt_rel[wpar2] = ("v", ptv)
                                    ptv_of[w0] = ("v", ptv)
                                else:
                                    P.wait("scalar", "t", tw)
                                    ptv = P.emit("scalar", lambda e, wp=wpar2, c=cn:
                                                 e.activation(PTb[wp][:, 0:c * 128],
                                                              ps_pt[wp][:, 0:c * 128],
                                                              AF.Copy), "s", 1)
                                    pt_rel[wpar2] = ("s", ptv)
                                    ptv_of[w0] = ("s", ptv)

                            def emit_PV(w0, first):
                                wpar2 = w0 % 2
                                cn = min(8, nkchunks - w0 * 8)
                                P.wait("tensor", ptv_of[w0][0], ptv_of[w0][1])
                                for i in range(cn):
                                    kc = w0 * 8 + i
                                    last = (kc == nkchunks - 1)
                                    P.emit("tensor",
                                           lambda e, kc0=kc, i0=i, wp=wpar2,
                                           st0=(first and i == 0), lst=last, b=Vb[par]:
                                           e.matmul(ps_o[:],
                                                    PTb[wp][:, i0 * 128:(i0 + 1) * 128],
                                                    b[:, kc0 * 128:(kc0 + 1) * 128],
                                                    start=st0, stop=lst),
                                           "t", 1 if last else None)

                            emit_T(0)
                            state["ptdone%d" % tp] = P.cnt["t"]
                            if state.get("o_rel"):
                                P.wait("tensor", "v", state["o_rel"])
                            for w0 in range(nw):
                                if w0 + 1 < nw:
                                    emit_T(w0 + 1)
                                    if w0 + 1 == nw - 1:
                                        state["ptdone%d" % tp] = P.cnt["t"]
                                emit_PV(w0, w0 == 0)
                            pv_t = P.cnt["t"]
                            P.wait("vector", "t", pv_t)
                            state["o_rel"] = P.emit("vector", lambda e, tp0=tp: e.tensor_scalar(
                                obuf[:], ps_o[:], linv[tp0][:], None, ALU.mult), "v", 1)
                            P.wait("tensor", "v", state["o_rel"])
                            if t0 in (0, 8) and state.get("ao_evac"):
                                P.wait("tensor", "v", state["ao_evac"])
                            P.emit("tensor", lambda e, c0=(t0 % 8) * 128: e.transpose(
                                ps_ao[:, c0:c0 + 128], obuf[:], id16[:]), "t", 1)
                            if t0 in (7, 15):
                                tcur = P.cnt["t"]
                                P.wait("vector", "t", tcur)
                                if t0 == 7 and aob_rel:
                                    P.wait("vector", "sta", aob_rel)
                                state["ao_evac"] = P.emit("vector",
                                                          lambda e, c0=(t0 // 8) * 1024:
                                                          e.tensor_copy(aob[:, c0:c0 + 1024],
                                                                        ps_ao[:]), "v", 1)

                        state["o_rel"] = o_rel_c[0]
                        state["ao_evac"] = ao_evac_c[0]
                        state["recip0"] = recip_c[0]
                        state["recip1"] = recip_c[1]
                        for t0 in range(16):
                            emit_sphase(t0)
                            if t0 > 0:
                                emit_ptpv(t0 - 1)
                        emit_ptpv(15)
                        o_rel_c[0] = state["o_rel"]
                        ao_evac_c[0] = state["ao_evac"]
                        recip_c[0] = state["recip0"]
                        recip_c[1] = state["recip1"]
                        head_rel[par] = P.cnt["t"]
                        P.wait("sync", "v", ao_evac_c[0])
                        aob_rel = P.emit("sync", lambda e, h0=h: e.dma_start(
                            out=aostage[h0 * 128:(h0 + 1) * 128, :], in_=aob[:]), "sta", 16)
                P.barrier()

            with contextlib.ExitStack() as prs:
                psb = lambda n, s, d: prs.enter_context(nc.sbuf_tensor(_nm(n), s, d))
                aor = psb("pr_aor", [128, NKC * T], BF16)
                wpb = [psb("pr_wp0", [128, 2048], BF16), psb("pr_wp1", [128, 2048], BF16)]
                x2c = [psb("pr_x2c0", [128, T], F32), psb("pr_x2c1", [128, T], F32)]
                x1c = [psb("pr_x1c0", [128, T], F32), psb("pr_x1c1", [128, T], F32)]
                P.emit("sync", lambda e: e.dma_start(
                    out=aor.ap().rearrange("p (k t) -> p k t", k=NKC),
                    in_=aostage.rearrange("(k p) t -> p k t", p=128)), "qst", 16)
                aor_ld = P.cnt["qst"]
                with contextlib.ExitStack() as pes:
                    ps_pr = [pes.enter_context(nc.psum_tensor(_nm("ps_pr%d" % i), [128, 512], F32))
                             for i in range(2)]
                    P.wait("tensor", "qst", aor_ld)
                    wp_use = [0, 0]
                    pr_rel = {0: 0, 1: 0}
                    x1_ld = [0, 0]
                    x2_rel = [0, 0]
                    x2_st = [0, 0]
                    for d0 in range(16):
                        wpar = d0 % 2
                        sg = "wa%d" % wpar
                        if wp_use[wpar]:
                            P.wait("gpsimd", "t", wp_use[wpar])
                        lw = P.emit("gpsimd", lambda e,
                                    b=wpb[wpar].ap().rearrange("p (k i) -> p k i", k=16),
                                    a=wproj[:, d0 * 128:(d0 + 1) * 128].rearrange(
                                        "(k p) i -> p k i", p=128):
                                    e.dma_start(out=b, in_=a), sg, 16)
                        if x2_rel[wpar]:
                            P.wait("sync", "v", x2_rel[wpar])
                        x1_ld[wpar] = P.emit("sync", lambda e, b=x1c[wpar], d1=d0:
                                             e.dma_start(out=b[:],
                                                         in_=x1d[d1 * 128:(d1 + 1) * 128, :]),
                                             "xl%d" % wpar, 16)
                        P.wait("tensor", sg, lw)
                        if x2_st[wpar]:
                            P.wait("vector", "stp%d" % wpar, x2_st[wpar])
                        for cb in range(4):
                            pp = cb % 2
                            if pr_rel[pp]:
                                P.wait("tensor", "v", pr_rel[pp])
                            for k in range(NKC):
                                P.emit("tensor",
                                       lambda e, k0=k, cb0=cb, b=wpb[wpar], pp0=pp:
                                       e.matmul(ps_pr[pp0][:], b[:, k0 * 128:(k0 + 1) * 128],
                                                aor[:, k0 * T + cb0 * 512: k0 * T + cb0 * 512 + 512],
                                                start=(k0 == 0), stop=(k0 == NKC - 1)),
                                       "t", 1 if k == NKC - 1 else None)
                            mmt = P.cnt["t"]
                            P.wait("vector", "t", mmt)
                            pr_rel[pp] = P.emit("vector",
                                                lambda e, cb0=cb, pp0=pp, d1=d0, wp=wpar:
                                                e.tensor_scalar(x2c[wp][:, cb0 * 512:(cb0 + 1) * 512],
                                                                ps_pr[pp0][:], bproj_s[:, d1:d1 + 1],
                                                                None, ALU.add), "v", 1)
                        wp_use[wpar] = P.cnt["t"]
                        P.wait("vector", "xl%d" % wpar, x1_ld[wpar])
                        vv = P.emit("vector", lambda e, wp=wpar: e.tensor_tensor(
                            x2c[wp][:], x2c[wp][:], x1c[wp][:], ALU.add), "v", 1)
                        x2_rel[wpar] = vv
                        P.wait("sync", "v", vv)
                        x2_st[wpar] = P.emit("sync", lambda e, wp=wpar, d1=d0: e.dma_start(
                            out=x2d[d1 * 128:(d1 + 1) * 128, :], in_=x2c[wp][:]),
                            "stp%d" % wpar, 16)
                P.barrier()

        if upto == "C":
            emit_ffn(lambda k, h: x2d[k * 128:(k + 1) * 128, h * HT:(h + 1) * HT],
                     lambda k, h: outx[k * 128:(k + 1) * 128, h * HT:(h + 1) * HT],
                     "C")

        for nm, (dbg, src) in dump_outs.items():
            P.barrier(["sync"])
            P.emit("sync", lambda e, d=dbg, s0=src: e.dma_start(out=d, in_=s0), "qst", 16)

        for s_ in ("qst", "qgst", "stq0", "stq1", "stv0", "stv1", "sta",
                   "stx0", "stx1", "stp0", "stp1"):
            P.wait("sync", s_)

        block = es.enter_context(nc.Block())

        @block.sync
        def _(e):
            for f_ in P.ops["sync"]:
                f_(e)

        @block.gpsimd
        def _(e):
            for f_ in P.ops["gpsimd"]:
                f_(e)

        @block.tensor
        def _(e):
            for f_ in P.ops["tensor"]:
                f_(e)

        @block.vector
        def _(e):
            for f_ in P.ops["vector"]:
                f_(e)

        @block.scalar
        def _(e):
            for f_ in P.ops["scalar"]:
                f_(e)

    return nc



def _prep_inputs(L0_attn_x, L0_attn_Wqkv, L0_attn_bqkv, L0_attn_Wproj,
                 L0_attn_bproj, L0_ffn_w_gate, L0_ffn_w_up, L0_ffn_w_down):
    x = np.asarray(L0_attn_x, np.float32)
    wqkv_ = np.ascontiguousarray(np.asarray(L0_attn_Wqkv, np.float32))
    wproj_ = np.ascontiguousarray(np.asarray(L0_attn_Wproj, np.float32))
    bqkv_ = np.asarray(L0_attn_bqkv, np.float32)
    bproj_ = np.asarray(L0_attn_bproj, np.float32)
    wgf = np.asarray(L0_ffn_w_gate, np.float32)
    wuf = np.asarray(L0_ffn_w_up, np.float32)
    wdf = np.asarray(L0_ffn_w_down, np.float32)

    wg_p = np.ascontiguousarray(
        wgf.reshape(2048, 16, 128, 4).transpose(0, 1, 3, 2).reshape(2048, 8192))
    wu_p = np.ascontiguousarray(
        wuf.reshape(2048, 16, 128, 4).transpose(0, 1, 3, 2).reshape(2048, 8192))
    wd_p = np.ascontiguousarray(
        wdf.reshape(16, 128, 4, 2048).transpose(0, 2, 1, 3).reshape(8192, 2048))
    bqkv2d = np.ascontiguousarray(bqkv_.reshape(48, 128).T)
    bproj2d = np.ascontiguousarray(bproj_.reshape(16, 128).T)

    in_maps = []
    tok_ids_all = []
    for core in range(8):
        seq, j = core // 4, core % 4
        tok = (np.arange(16)[:, None] * 512 + j * 128 + np.arange(128)[None, :]).reshape(-1)
        tok_ids_all.append((seq, tok))
        x0f = np.ascontiguousarray(x[seq][tok, :].T)
        q = np.arange(128)[:, None]
        w = np.arange(512)[None, :]
        mask = np.where(w <= j * 128 + q, 0.0, NEGBIG).astype(np.float32)
        in_maps.append({
            "x0": x0f, "wqkv": wqkv_, "wproj": wproj_, "wg": wg_p, "wu": wu_p,
            "wd": wd_p, "bqkv": bqkv2d, "bproj": bproj2d, "maskin": mask,
        })
    return in_maps, tok_ids_all


def kernel(L0_attn_x, L0_attn_Wqkv, L0_attn_bqkv, L0_attn_Wproj, L0_attn_bproj,
           L0_ffn_w_gate, L0_ffn_w_up, L0_ffn_w_down):
    in_maps, tok_ids_all = _prep_inputs(
        L0_attn_x, L0_attn_Wqkv, L0_attn_bqkv, L0_attn_Wproj, L0_attn_bproj,
        L0_ffn_w_gate, L0_ffn_w_up, L0_ffn_w_down)
    nc = build_program("C")
    res = run_bass_kernel_spmd(nc, in_maps, core_ids=list(range(8)))
    out = np.zeros((2, 8192, 2048), np.float32)
    for core in range(8):
        seq, tok = tok_ids_all[core]
        out[seq][tok, :] = np.asarray(res.results[core]["outx"], np.float32).T
    return out


# version 08
# speedup vs baseline: 1.0114x; 1.0114x over previous
import contextlib

import numpy as np
import ml_dtypes

import concourse.bass as bass
import concourse.mybir as mybir
from concourse.bass_utils import run_bass_kernel_spmd

F32 = mybir.dt.float32
BF16 = mybir.dt.bfloat16
AF = mybir.ActivationFunctionType
ALU = mybir.AluOpType
AX = mybir.AxisListType

D = 2048
T = 2048
HT = 1024
NI = 8192
NH = 16
NKC = D // 128
EPS = 1e-5
NEGBIG = -1.0e30
QSC = 1.0 / float(np.sqrt(128.0))

ENGS = ("sync", "gpsimd", "tensor", "vector", "scalar")


class Plan:
    def __init__(self, nc, es):
        self.nc = nc
        self.ops = {e: [] for e in ENGS}
        self.sem = {}
        self.cnt = {}
        self.es = es

    def new_sem(self, name):
        self.sem[name] = self.es.enter_context(self.nc.semaphore(name))
        self.cnt[name] = 0

    def wait(self, eng, name, val=None):
        if val is None:
            val = self.cnt[name]
        if val <= 0:
            return
        h = self.sem[name]
        self.ops[eng].append(lambda e, h=h, v=val: e.wait_ge(h, v))

    def emit(self, eng, fn, sem=None, amt=1):
        if sem is None or amt is None:
            self.ops[eng].append(fn)
            return None
        h = self.sem[sem]
        self.cnt[sem] += amt
        self.ops[eng].append(lambda e, fn=fn, h=h, a=amt: fn(e).then_inc(h, a))
        return self.cnt[sem]

    def barrier(self, engs=ENGS):
        snap = dict(self.cnt)
        for eng in engs:
            for name, v in snap.items():
                self.wait(eng, name, v)


def build_program(upto="C", dumps=(), dbg_ffn=False):
    nc = bass.Bass("TRN2", target_bir_lowering=False, debug=False, num_devices=8)

    x0 = nc.dram_tensor("x0", [D, T], F32, kind="ExternalInput").ap()
    wqkv = nc.dram_tensor("wqkv", [D, 3 * D], F32, kind="ExternalInput").ap()
    wproj = nc.dram_tensor("wproj", [D, D], F32, kind="ExternalInput").ap()
    wg = nc.dram_tensor("wg", [D, NI], F32, kind="ExternalInput").ap()
    wu = nc.dram_tensor("wu", [D, NI], F32, kind="ExternalInput").ap()
    wd = nc.dram_tensor("wd", [NI, D], F32, kind="ExternalInput").ap()
    bqkv = nc.dram_tensor("bqkv", [128, 48], F32, kind="ExternalInput").ap()
    bproj = nc.dram_tensor("bproj", [128, 16], F32, kind="ExternalInput").ap()
    maskin = nc.dram_tensor("maskin", [128, 512], F32, kind="ExternalInput").ap()
    outx = nc.dram_tensor("outx", [D, T], F32, kind="ExternalOutput").ap()

    x1d = nc.dram_tensor("x1d", [D, T], F32, kind="Internal").ap()
    x2d = nc.dram_tensor("x2d", [D, T], F32, kind="Internal").ap()
    qstage = nc.dram_tensor("qstage", [D, T], BF16, kind="Internal").ap()
    kstage = nc.dram_tensor("kstage", [D, T], BF16, kind="Internal").ap()
    vstage = nc.dram_tensor("vstage", [T, D], BF16, kind="Internal").ap()
    kall = nc.dram_tensor("kall", [4 * D, T], BF16, kind="Internal").ap()
    vall = nc.dram_tensor("vall", [4 * T, D], BF16, kind="Internal").ap()
    aostage = nc.dram_tensor("aostage", [D, T], BF16, kind="Internal").ap()
    wg16 = nc.dram_tensor("wg16", [64, 128, 2048], BF16, kind="Internal").ap()
    wu16 = nc.dram_tensor("wu16", [64, 128, 2048], BF16, kind="Internal").ap()
    wd16 = nc.dram_tensor("wd16", [64, 128, 2048], BF16, kind="Internal").ap()

    dump_outs = {}
    for nm in dumps:
        src = {"x1": (x1d, F32, [D, T]), "x2": (x2d, F32, [D, T]),
               "kall": (kall, BF16, [4 * D, T]), "vall": (vall, BF16, [4 * T, D]),
               "q": (qstage, BF16, [D, T]), "ao": (aostage, BF16, [D, T]),
               "vstage": (vstage, BF16, [T, D]),
               "wg16": (wg16, BF16, [64, 128, 2048]),
               "wu16": (wu16, BF16, [64, 128, 2048]),
               "wd16": (wd16, BF16, [64, 128, 2048])}[nm]
        dump_outs[nm] = (nc.dram_tensor("dbg_" + nm, src[2], src[1],
                                        kind="ExternalOutput").ap(), src[0])

    if dbg_ffn:
        dbg_xn = nc.dram_tensor("dbg_xn", [128, NKC * HT], BF16, kind="ExternalOutput").ap()
        dbg_rows = nc.dram_tensor("dbg_rows", [2, HT], F32, kind="ExternalOutput").ap()
        dbg_acc = nc.dram_tensor("dbg_acc", [128, NKC * HT], F32, kind="ExternalOutput").ap()
    id_dram = nc.inline_tensor(np.eye(128, dtype=ml_dtypes.bfloat16), "id16d")

    _uid = [0]

    def _nm(n):
        _uid[0] += 1
        return "%s_%d" % (n, _uid[0])

    es = contextlib.ExitStack()
    with es:
        sbp = lambda n, s, d: es.enter_context(nc.sbuf_tensor(_nm(n), s, d))
        id16 = sbp("id16", [128, 128], BF16)
        bqkv_s = sbp("bqkv_s", [128, 48], F32)
        bqs_s = sbp("bqs_s", [128, 16], F32)
        bproj_s = sbp("bproj_s", [128, 16], F32)
        mask_s = sbp("mask_s", [128, 512], F32)
        ones_c = sbp("ones_c", [128, 1], F32)
        ones_r = sbp("ones_r", [1, 128], F32)
        r_a = sbp("r_a", [1, 1024], F32)
        r_b = sbp("r_b", [1, 1024], F32)
        rstd_row = sbp("rstd_row", [1, 1024], F32)
        nm_row = sbp("nm_row", [1, 1024], F32)

        P = Plan(nc, es)
        for s_ in ("qst", "qgst", "xl0", "xl1", "wa0", "wa1", "wb0", "wb1",
                   "wc0", "wc1", "kv0", "kv1", "cc", "t", "v", "s", "g",
                   "stq0", "stq1", "stv0", "stv1", "sta", "stx0", "stx1",
                   "stp0", "stp1"):
            P.new_sem(s_)

        P.emit("sync", lambda e: e.dma_start(out=id16[:], in_=id_dram.ap()), "xl0", 16)
        P.emit("sync", lambda e: e.dma_start(out=bqkv_s[:], in_=bqkv), "xl0", 16)
        P.emit("sync", lambda e: e.dma_start(out=bproj_s[:], in_=bproj), "xl0", 16)
        P.emit("sync", lambda e: e.dma_start(out=mask_s[:], in_=maskin), "xl0", 16)
        P.emit("gpsimd", lambda e: e.memset(ones_c[:], 1.0), "g", 1)
        P.emit("gpsimd", lambda e: e.memset(ones_r[:], 1.0), "g", 1)
        P.wait("vector", "xl0")
        P.emit("vector", lambda e: e.tensor_scalar(
            bqs_s[:], bqkv_s[:, 0:16], QSC, None, ALU.mult), "v", 1)
        P.barrier()


        def emit_stats_half(xsrc_fn, col0, xbufs, sqbufs, W=HT):
            with contextlib.ExitStack() as les:
                ps1 = les.enter_context(nc.psum_tensor(_nm("ps_s1"), [1, W], F32))
                ps2 = les.enter_context(nc.psum_tensor(_nm("ps_s2"), [1, W], F32))
                xb = [b[:, 0:W] for b in xbufs]
                sq = [b[:, 0:W] for b in sqbufs]
                P.wait("sync", "t")
                P.wait("sync", "s")
                P.wait("sync", "v")
                use_t = [0, 0]
                for k in range(NKC):
                    par = k % 2
                    if use_t[par]:
                        P.wait("sync", "t", use_t[par])
                    ld = P.emit("sync", lambda e, a=xsrc_fn(k), b=xb[par]:
                                e.dma_start(out=b, in_=a), "xl%d" % par, 16)
                    P.wait("scalar", "xl%d" % par, ld)
                    if use_t[par]:
                        P.wait("scalar", "t", use_t[par])
                    sc = P.emit("scalar", lambda e, i=xb[par], o=sq[par]:
                                e.activation(o, i, AF.Square), "s", 1)
                    P.wait("tensor", "xl%d" % par, ld)
                    for cb in range(W // 512):
                        P.emit("tensor", lambda e, r=xb[par], p=ps1, c=cb, k0=k:
                               e.matmul(p[:, c * 512:(c + 1) * 512], ones_c[:],
                                        r[:, c * 512:(c + 1) * 512],
                                        start=(k0 == 0), stop=(k0 == NKC - 1)),
                               "t", 1)
                    P.wait("tensor", "s", sc)
                    for cb in range(W // 512):
                        P.emit("tensor", lambda e, r=sq[par], p=ps2, c=cb, k0=k:
                               e.matmul(p[:, c * 512:(c + 1) * 512], ones_c[:],
                                        r[:, c * 512:(c + 1) * 512],
                                        start=(k0 == 0), stop=(k0 == NKC - 1)),
                               "t", 1)
                    use_t[par] = P.cnt["t"]
                P.wait("vector", "t")
                P.emit("vector", lambda e: e.tensor_scalar(
                    r_a[:, 0:W], ps1[:], 1.0 / D, None, ALU.mult), "v", 1)
                P.emit("vector", lambda e: e.tensor_scalar(
                    r_b[:, 0:W], ps2[:], 1.0 / D, None, ALU.mult), "v", 1)
                P.emit("vector", lambda e: e.tensor_tensor(
                    rstd_row[:, col0:col0 + W], r_a[:, 0:W], r_a[:, 0:W], ALU.mult), "v", 1)
                P.emit("vector", lambda e: e.tensor_tensor(
                    r_b[:, 0:W], r_b[:, 0:W], rstd_row[:, col0:col0 + W],
                    ALU.subtract), "v", 1)
                P.emit("vector", lambda e: e.tensor_scalar(
                    r_b[:, 0:W], r_b[:, 0:W], 1.0, EPS, ALU.mult, ALU.add), "v", 1)
                P.wait("scalar", "v")
                P.emit("scalar", lambda e: e.activation(
                    r_b[:, 0:W], r_b[:, 0:W], AF.Sqrt), "s", 1)
                P.wait("vector", "s")
                P.emit("vector", lambda e: e.reciprocal(
                    rstd_row[:, col0:col0 + W], r_b[:, 0:W]), "v", 1)
                P.emit("vector", lambda e: e.scalar_tensor_tensor(
                    nm_row[:, col0:col0 + W], r_a[:, 0:W], -1.0,
                    rstd_row[:, col0:col0 + W], ALU.mult, ALU.mult), "v", 1)
                P.wait("tensor", "v")

        def emit_bcast(col0, W, Rb, Mb):
            with contextlib.ExitStack() as les:
                nb = W // 512
                pbs = [les.enter_context(nc.psum_tensor(_nm("ps_b%d" % i), [128, 512], F32))
                       for i in range(min(nb, 4))]
                P.wait("tensor", "v")
                for i in range(nb):
                    P.emit("tensor", lambda e, i=i, p=pbs[i % len(pbs)]:
                           e.matmul(p[:], ones_r[:],
                                    rstd_row[0:1, col0 + i * 512: col0 + (i + 1) * 512],
                                    start=True, stop=True), "t", 1)
                P.wait("vector", "t")
                for i in range(nb):
                    P.emit("vector", lambda e, i=i, p=pbs[i % len(pbs)]:
                           e.tensor_copy(Rb[:, i * 512:(i + 1) * 512], p[:]), "v", 1)
                P.wait("tensor", "v")
                for i in range(nb):
                    P.emit("tensor", lambda e, i=i, p=pbs[i % len(pbs)]:
                           e.matmul(p[:], ones_r[:],
                                    nm_row[0:1, col0 + i * 512: col0 + (i + 1) * 512],
                                    start=True, stop=True), "t", 1)
                P.wait("vector", "t")
                for i in range(nb):
                    P.emit("vector", lambda e, i=i, p=pbs[i % len(pbs)]:
                           e.tensor_copy(Mb[:, i * 512:(i + 1) * 512], p[:]), "v", 1)
                P.wait("tensor", "v")

        def emit_apply(xsrc_fn, xn, Rb, Mb, W, tmpf, xb, xbsem):
            P.wait("sync", "t")
            P.wait("sync", "s")
            P.wait("sync", "v")
            use_v = [0, 0]
            for k in range(NKC):
                par = k % 2
                if use_v[par]:
                    P.wait("sync", "v", use_v[par])
                ld = P.emit("sync", lambda e, a=xsrc_fn(k), b=xb[par]:
                            e.dma_start(out=b[:], in_=a), xbsem[par], 16)
                P.wait("vector", xbsem[par], ld)
                P.emit("vector", lambda e, b=xb[par]: e.tensor_tensor(
                    tmpf[:, 0:W], b[:, 0:W], Rb[:, 0:W], ALU.mult), "v", 1)
                use_v[par] = P.emit("vector", lambda e, k0=k: e.tensor_tensor(
                    xn[:, k0 * W:(k0 + 1) * W], tmpf[:, 0:W], Mb[:, 0:W],
                    ALU.add), "v", 1)


        def emit_ffn(xin_chunk, xout_chunk, app):
            with contextlib.ExitStack() as fes:
                fsb = lambda n, s, d: fes.enter_context(nc.sbuf_tensor(_nm(n), s, d))
                xn = fsb("ffn_xn", [128, NKC * HT], BF16)
                acc = fsb("ffn_acc", [128, NKC * HT], F32)
                slab = fsb("ffn_slab", [128, 8 * HT], BF16)
                Rb = fsb("ffn_Rb", [128, HT], F32)
                Mb = fsb("ffn_Mb", [128, HT], F32)
                tmpf = fsb("ffn_tmpf", [128, HT], F32)
                xb = [fsb("ffn_xb0", [128, HT], F32), fsb("ffn_xb1", [128, HT], F32)]
                wgb = [fsb("wgb0", [128, 2048], BF16), fsb("wgb1", [128, 2048], BF16)]
                wub = [fsb("wub0", [128, 2048], BF16), fsb("wub1", [128, 2048], BF16)]
                wdb = [fsb("wdb0", [128, 2048], BF16), fsb("wdb1", [128, 2048], BF16)]
                hq = [fsb("hq%d" % r, [128, HT], BF16) for r in range(4)]
                na = [fsb("na%d" % r, [128, HT], BF16) for r in range(4)]
                tA = fsb("tA", [128, HT], BF16)
                tC = fsb("tC", [128, HT], BF16)
                tD = fsb("tD", [128, HT], BF16)
                x1b = [fsb("x1b0", [128, HT], F32), fsb("x1b1", [128, HT], F32)]

                wg_use = [0, 0]
                wd_use = [0, 0]

                for half in range(2):
                    fresh = (app == "A" and half == 0)

                    def wsrc(which, ic):
                        if fresh:
                            if which == "d":
                                g0, dp = ic // 8, ic % 8
                                return wd[g0 * 1024:(g0 + 1) * 1024,
                                          dp * 256:(dp + 1) * 256].rearrange(
                                              "(s p) d -> p s d", p=128)
                            src = wg if which == "g" else wu
                            return src[:, ic * 128:(ic + 1) * 128].rearrange(
                                "(k p) i -> p k i", p=128)
                        cache = {"g": wg16, "u": wu16, "d": wd16}[which]
                        return cache[ic]

                    emit_stats_half(lambda k: xin_chunk(k, half), 0, xb, x1b, HT)
                    if dbg_ffn and app == "A" and half == 0:
                        P.wait("sync", "v")
                        P.emit("sync", lambda e: e.dma_start(
                            out=dbg_rows[0:1, :], in_=rstd_row[:]), "qst", 16)
                        P.emit("sync", lambda e: e.dma_start(
                            out=dbg_rows[1:2, :], in_=nm_row[:]), "qst", 16)
                    emit_bcast(0, HT, Rb, Mb)
                    emit_apply(lambda k: xin_chunk(k, half), xn, Rb, Mb, HT,
                               tmpf, xb, ("xl0", "xl1"))
                    xn_done_v = P.cnt["v"]
                    if dbg_ffn and app == "A" and half == 0:
                        P.wait("sync", "v", xn_done_v)
                        P.emit("sync", lambda e: e.dma_start(
                            out=dbg_xn, in_=xn[:]), "qst", 16)

                    with contextlib.ExitStack() as pes:
                        ps_g = [pes.enter_context(nc.psum_tensor(_nm("ps_g%d" % i), [128, 512], F32))
                                for i in range(2)]
                        ps_u = [pes.enter_context(nc.psum_tensor(_nm("ps_u%d" % i), [128, 512], F32))
                                for i in range(2)]
                        ps_d = [pes.enter_context(nc.psum_tensor(_nm("ps_d%d" % i), [128, 512], F32))
                                for i in range(4)]
                        P.wait("tensor", "v", xn_done_v)
                        gate_rel = {0: 0, 1: 0}
                        up_rel = {0: 0, 1: 0}
                        dn_rel = {0: 0, 1: 0, 2: 0, 3: 0}
                        slab_free_t = 0
                        hq_free_v = 0

                        for g0 in range(8):
                            for ccq in range(2):
                                cc = g0 * 2 + ccq
                                for r in range(4):
                                    ic = 4 * cc + r
                                    wpar = (cc * 4 + r) % 2
                                    sg, su = "wa%d" % wpar, "wb%d" % wpar
                                    gq = "gpsimd" if fresh else "sync"
                                    uq = "gpsimd" if fresh else "scalar"
                                    if wg_use[wpar]:
                                        P.wait(gq, "t", wg_use[wpar])
                                        P.wait(uq, "t", wg_use[wpar])
                                    if fresh:
                                        P.wait("gpsimd", "qgst")
                                    dstg = wgb[wpar].ap().rearrange("p (k i) -> p k i", k=16) \
                                        if fresh else wgb[wpar][:]
                                    lg = P.emit(gq, lambda e, a=wsrc("g", ic), b=dstg:
                                                e.dma_start(out=b, in_=a), sg, 16)
                                    if fresh:
                                        P.wait("scalar", sg, lg)
                                        P.emit("scalar", lambda e, b=wgb[wpar], o=wg16[ic]:
                                               e.dma_start(out=o, in_=b[:]), "qgst", 16)
                                    dstu = wub[wpar].ap().rearrange("p (k i) -> p k i", k=16) \
                                        if fresh else wub[wpar][:]
                                    lu = P.emit(uq, lambda e, a=wsrc("u", ic), b=dstu:
                                                e.dma_start(out=b, in_=a), su, 16)
                                    if fresh:
                                        P.wait("scalar", su, lu)
                                        P.emit("scalar", lambda e, b=wub[wpar], o=wu16[ic]:
                                               e.dma_start(out=o, in_=b[:]), "qgst", 16)
                                    P.wait("tensor", sg, lg)
                                    P.wait("tensor", su, lu)
                                    for tb in range(2):
                                        if gate_rel[tb]:
                                            P.wait("tensor", "s", gate_rel[tb])
                                        for k in range(NKC):
                                            P.emit("tensor",
                                                   lambda e, k0=k, pg=ps_g[tb], b=wgb[wpar], tb0=tb:
                                                   e.matmul(pg[:], b[:, k0 * 128:(k0 + 1) * 128],
                                                            xn[:, k0 * HT + tb0 * 512: k0 * HT + tb0 * 512 + 512],
                                                            start=(k0 == 0), stop=(k0 == NKC - 1)),
                                                   "t", 1 if k == NKC - 1 else None)
                                        if up_rel[tb]:
                                            P.wait("tensor", "v", up_rel[tb])
                                        for k in range(NKC):
                                            P.emit("tensor",
                                                   lambda e, k0=k, pu=ps_u[tb], b=wub[wpar], tb0=tb:
                                                   e.matmul(pu[:], b[:, k0 * 128:(k0 + 1) * 128],
                                                            xn[:, k0 * HT + tb0 * 512: k0 * HT + tb0 * 512 + 512],
                                                            start=(k0 == 0), stop=(k0 == NKC - 1)),
                                                   "t", 1 if k == NKC - 1 else None)
                                    wg_use[wpar] = P.cnt["t"]
                                    tcnt = P.cnt["t"]
                                    gs = {0: tcnt - 3, 1: tcnt - 1}
                                    us = {0: tcnt - 2, 1: tcnt}
                                    for tb in range(2):
                                        P.wait("scalar", "t", gs[tb])
                                        if hq_free_v:
                                            P.wait("scalar", "v", hq_free_v)
                                        sc = P.emit("scalar", lambda e, pg=ps_g[tb], r0=r, tb0=tb:
                                                    e.activation(hq[r0][:, tb0 * 512: tb0 * 512 + 512],
                                                                 pg[:], AF.Silu), "s", 1)
                                        gate_rel[tb] = sc
                                        P.wait("vector", "s", sc)
                                        P.wait("vector", "t", us[tb])
                                        mv = P.emit("vector", lambda e, pu=ps_u[tb], r0=r, tb0=tb:
                                                    e.tensor_tensor(
                                                        hq[r0][:, tb0 * 512: tb0 * 512 + 512],
                                                        hq[r0][:, tb0 * 512: tb0 * 512 + 512],
                                                        pu[:], ALU.mult), "v", 1)
                                        up_rel[tb] = mv
                                if slab_free_t:
                                    P.wait("vector", "t", slab_free_t)
                                for r in range(4):
                                    P.emit("vector", lambda e, r0=r: e.scalar_tensor_tensor(
                                        na[r0][:], hq[r0][:], -1.0, hq[r0][:],
                                        ALU.mult, ALU.min))
                                P.emit("vector", lambda e: e.tensor_tensor(tA[:], na[0][:], na[1][:], ALU.min))
                                P.emit("vector", lambda e: e.tensor_tensor(tC[:], na[2][:], na[3][:], ALU.min))
                                P.emit("vector", lambda e: e.tensor_tensor(tA[:], tA[:], tC[:], ALU.max))
                                P.emit("vector", lambda e: e.tensor_tensor(tC[:], na[0][:], na[1][:], ALU.max))
                                P.emit("vector", lambda e: e.tensor_tensor(tD[:], na[2][:], na[3][:], ALU.max))
                                P.emit("vector", lambda e: e.tensor_tensor(tC[:], tC[:], tD[:], ALU.min))
                                P.emit("vector", lambda e: e.tensor_tensor(tA[:], tA[:], tC[:], ALU.min))
                                for r in range(4):
                                    P.emit("vector", lambda e, r0=r: e.tensor_tensor(
                                        tD[:], na[r0][:], tA[:], ALU.is_le))
                                    slot = (cc - 2 * g0) * 4 + r
                                    hq_free_v = P.emit("vector", lambda e, r0=r, s0=slot:
                                                       e.tensor_tensor(
                                                           slab[:, s0 * HT:(s0 + 1) * HT],
                                                           hq[r0][:], tD[:], ALU.mult), "v", 1)
                            net_done_v = P.cnt["v"]
                            P.wait("tensor", "v", net_done_v)
                            for dp in range(8):
                                wpar = dp % 2
                                sd = "wc%d" % wpar
                                dq = "gpsimd" if fresh else ("sync" if wpar == 0 else "scalar")
                                if wd_use[wpar]:
                                    P.wait(dq, "t", wd_use[wpar])
                                if fresh:
                                    P.wait("gpsimd", "qgst")
                                dstd = wdb[wpar].ap().rearrange("p (s d) -> p s d", s=8) \
                                    if fresh else wdb[wpar][:]
                                ldd = P.emit(dq, lambda e, a=wsrc("d", g0 * 8 + dp), b=dstd:
                                             e.dma_start(out=b, in_=a), sd, 16)
                                if fresh:
                                    P.wait("scalar", sd, ldd)
                                    P.emit("scalar", lambda e, b=wdb[wpar], o=wd16[g0 * 8 + dp]:
                                           e.dma_start(out=o, in_=b[:]), "qgst", 16)
                                P.wait("tensor", sd, ldd)
                                for dd in range(2):
                                    d0 = 2 * dp + dd
                                    for tb in range(2):
                                        dpi = dd * 2 + tb
                                        if dn_rel[dpi]:
                                            P.wait("tensor", "v", dn_rel[dpi])
                                        for s0 in range(8):
                                            P.emit("tensor",
                                                   lambda e, s1=s0, pd=ps_d[dpi], b=wdb[wpar],
                                                   tb0=tb, dd0=dd:
                                                   e.matmul(pd[:], b[:, s1 * 256 + dd0 * 128: s1 * 256 + dd0 * 128 + 128],
                                                            slab[:, s1 * HT + tb0 * 512: s1 * HT + tb0 * 512 + 512],
                                                            start=(s1 == 0), stop=(s1 == 7)),
                                                   "t", 1 if s0 == 7 else None)
                                        mmc = P.cnt["t"]
                                        P.wait("vector", "t", mmc)
                                        if g0 == 0:
                                            dn_rel[dpi] = P.emit(
                                                "vector", lambda e, pd=ps_d[dpi], d1=d0, tb0=tb:
                                                e.tensor_copy(
                                                    acc[:, d1 * HT + tb0 * 512: d1 * HT + tb0 * 512 + 512],
                                                    pd[:]), "v", 1)
                                        else:
                                            dn_rel[dpi] = P.emit(
                                                "vector", lambda e, pd=ps_d[dpi], d1=d0, tb0=tb:
                                                e.tensor_tensor(
                                                    acc[:, d1 * HT + tb0 * 512: d1 * HT + tb0 * 512 + 512],
                                                    acc[:, d1 * HT + tb0 * 512: d1 * HT + tb0 * 512 + 512],
                                                    pd[:], ALU.add), "v", 1)
                                wd_use[wpar] = P.cnt["t"]
                            slab_free_t = P.cnt["t"]

                    if dbg_ffn and app == "A" and half == 0:
                        P.wait("sync", "v")
                        P.emit("sync", lambda e: e.dma_start(
                            out=dbg_acc, in_=acc[:]), "qst", 16)
                        P.wait("sync", "qst")
                    P.wait("sync", "t")
                    P.wait("sync", "s")
                    P.wait("sync", "v")
                    usef = [0, 0]
                    stcnt = [0, 0]
                    for k in range(NKC):
                        par = k % 2
                        if usef[par]:
                            P.wait("sync", "v", usef[par])
                        ldf = P.emit("sync", lambda e, a=xin_chunk(k, half), b=xb[par]:
                                     e.dma_start(out=b[:], in_=a), "xl%d" % par, 16)
                        P.wait("vector", "xl%d" % par, ldf)
                        if stcnt[par]:
                            P.wait("vector", "stx%d" % par, stcnt[par])
                        usef[par] = P.emit("vector", lambda e, k0=k, b=xb[par], o=x1b[par]:
                                           e.scalar_tensor_tensor(
                                               o[:], acc[:, k0 * HT:(k0 + 1) * HT], 0.5,
                                               b[:], ALU.mult, ALU.add), "v", 1)
                        P.wait("sync", "v", usef[par])
                        stcnt[par] = P.emit("sync", lambda e, o=x1b[par], dst=xout_chunk(k, half):
                                            e.dma_start(out=dst, in_=o[:]), "stx%d" % par, 16)
                    P.barrier()


        if upto in ("A", "A2", "B", "C"):
            emit_ffn(lambda k, h: x0[k * 128:(k + 1) * 128, h * HT:(h + 1) * HT],
                     lambda k, h: x1d[k * 128:(k + 1) * 128, h * HT:(h + 1) * HT],
                     "A")

        if upto in ("A2", "B", "C"):
            with contextlib.ExitStack() as aes:
                asb = lambda n, s, d: aes.enter_context(nc.sbuf_tensor(_nm(n), s, d))
                Rb2 = asb("a2_Rb", [128, T], F32)
                Mb2 = asb("a2_Mb", [128, T], F32)
                tmp2 = asb("a2_tmp", [128, T], F32)
                xb2 = [asb("a2_xb0", [128, T], F32), asb("a2_xb1", [128, T], F32)]
                a2sq = asb("a2_sq", [128, HT], F32)
                xn2 = asb("a2_xn", [128, NKC * T], BF16)
                for half in range(2):
                    emit_stats_half(
                        lambda k, h=half: x1d[k * 128:(k + 1) * 128, h * HT:(h + 1) * HT],
                        0, xb2, [tmp2, a2sq], HT)
                    emit_bcast(0, HT, Rb2[:, half * HT:(half + 1) * HT],
                               Mb2[:, half * HT:(half + 1) * HT])
                emit_apply(lambda k: x1d[k * 128:(k + 1) * 128, :], xn2, Rb2, Mb2, T,
                           tmp2, xb2, ("xl0", "xl1"))
                xn2_done = P.cnt["v"]
                qsg = [asb("a2_qs0", [128, T], BF16), asb("a2_qs1", [128, T], BF16)]
                vtb = [asb("a2_vt0", [128, 1024], BF16), asb("a2_vt1", [128, 1024], BF16)]
                wqb = [asb("a2_wq0", [128, 2048], BF16), asb("a2_wq1", [128, 2048], BF16)]
                with contextlib.ExitStack() as pes:
                    ps_q = [pes.enter_context(nc.psum_tensor(_nm("ps_q%d" % i), [128, 1024], F32))
                            for i in range(3)]
                    ps_vt = [pes.enter_context(nc.psum_tensor(_nm("ps_vt%d" % i), [128, 1024], BF16))
                             for i in range(2)]
                    P.wait("tensor", "v", xn2_done)
                    wq_use = [0, 0]
                    q_rel = {0: 0, 1: 0, 2: 0}
                    qslot = [0]
                    stg_rel = [0, 0]
                    stg_rel_t = [0, 0]
                    vt_rel = {0: 0, 1: 0}
                    vtb_rel = [0, 0]
                    forder = list(range(16, 48)) + list(range(16))
                    for fi, f in enumerate(forder):
                        wpar = fi % 2
                        sg = "wa%d" % wpar
                        if wq_use[wpar]:
                            P.wait("gpsimd", "t", wq_use[wpar])
                        lq = P.emit("gpsimd", lambda e,
                                    b=wqb[wpar].ap().rearrange("p (k i) -> p k i", k=16),
                                    a=wqkv[:, f * 128:(f + 1) * 128].rearrange(
                                        "(k p) i -> p k i", p=128):
                                    e.dma_start(out=b, in_=a), sg, 16)
                        P.wait("tensor", sg, lq)
                        slots = []
                        for hb in range(2):
                            sl = qslot[0]
                            qslot[0] = (qslot[0] + 1) % 3
                            slots.append(sl)
                            if q_rel[sl]:
                                P.wait("tensor", "v", q_rel[sl])
                            for cb in range(2):
                                for k in range(NKC):
                                    P.emit("tensor",
                                           lambda e, k0=k, hb0=hb, cb0=cb, b=wqb[wpar], sl0=sl:
                                           e.matmul(ps_q[sl0][:, cb0 * 512:(cb0 + 1) * 512],
                                                    b[:, k0 * 128:(k0 + 1) * 128],
                                                    xn2[:, k0 * T + hb0 * 1024 + cb0 * 512:
                                                        k0 * T + hb0 * 1024 + cb0 * 512 + 512],
                                                    start=(k0 == 0), stop=(k0 == NKC - 1)),
                                           "t", 1 if (k == NKC - 1 and cb == 1) else None)
                        wq_use[wpar] = P.cnt["t"]
                        tq = P.cnt["t"]
                        spar = fi % 2
                        if stg_rel[spar]:
                            P.wait("vector", "stq%d" % spar, stg_rel[spar])
                        if stg_rel_t[spar]:
                            P.wait("vector", "t", stg_rel_t[spar])
                        bias_ap = bqkv_s[:, f:f + 1]
                        scl = QSC if f < 16 else None
                        for hb in range(2):
                            sl = slots[hb]
                            P.wait("vector", "t", tq - 1 + hb)
                            q_rel[sl] = P.emit("vector",
                                               lambda e, hb0=hb, b=bias_ap, sc0=scl, sp=spar, sl0=sl:
                                               e.tensor_scalar(qsg[sp][:, hb0 * 1024:(hb0 + 1) * 1024],
                                                               ps_q[sl0][:], b, sc0, ALU.add,
                                                               ALU.mult) if sc0 is not None else
                                               e.tensor_scalar(qsg[sp][:, hb0 * 1024:(hb0 + 1) * 1024],
                                                               ps_q[sl0][:], b, None, ALU.add),
                                               "v", 1)
                        se = P.cnt["v"]
                        if f < 32:
                            dst = (qstage if f < 16 else kstage)
                            row = (f % 16) * 128
                            P.wait("sync", "v", se)
                            stg_rel[spar] = P.emit("sync", lambda e, sp=spar, d=dst, r0=row:
                                                   e.dma_start(out=d[r0:r0 + 128, :],
                                                               in_=qsg[sp][:]),
                                                   "stq%d" % spar, 16)
                        else:
                            h0 = f - 32
                            P.wait("tensor", "v", se)
                            for w0 in range(2):
                                if vt_rel[w0]:
                                    P.wait("tensor", vt_rel[w0][0], vt_rel[w0][1])
                                for tc in range(8):
                                    tcg = w0 * 8 + tc
                                    P.emit("tensor", lambda e, sp=spar, t0=tcg, w1=w0, tc0=tc:
                                           e.transpose(ps_vt[w1][:, tc0 * 128:(tc0 + 1) * 128],
                                                       qsg[sp][:, t0 * 128:(t0 + 1) * 128],
                                                       id16[:]),
                                           "t", 1 if tc == 7 else None)
                                tw = P.cnt["t"]
                                P.wait("scalar", "t", tw)
                                if vtb_rel[w0]:
                                    P.wait("scalar", "stv%d" % w0, vtb_rel[w0])
                                vt_rel_s = P.emit("scalar", lambda e, w1=w0:
                                                  e.activation(vtb[w1][:], ps_vt[w1][:],
                                                               AF.Copy), "s", 1)
                                vt_rel[w0] = ("s", vt_rel_s)
                                P.wait("sync", "s", vt_rel_s)
                                dstv = vstage[w0 * 1024: w0 * 1024 + 1024,
                                              h0 * 128:(h0 + 1) * 128].rearrange(
                                                  "(tc p) dh -> p tc dh", p=128)
                                srcv = vtb[w0].ap().rearrange("p (tc dh) -> p tc dh", tc=8)
                                vtb_rel[w0] = P.emit("sync", lambda e, dv=dstv, sv=srcv:
                                                     e.dma_start(out=dv, in_=sv),
                                                     "stv%d" % w0, 16)
                            stg_rel[spar] = P.cnt["stq%d" % spar]
                            stg_rel_t[spar] = P.cnt["t"]
                        if f == 31:
                            P.wait("gpsimd", "stq0")
                            P.wait("gpsimd", "stq1")
                            P.emit("gpsimd", lambda e: e.collective_compute(
                                "AllGather", ALU.bypass,
                                replica_groups=[[0, 1, 2, 3], [4, 5, 6, 7]],
                                ins=[kstage], outs=[kall]), "cc", 1)
                        if f == 47:
                            P.wait("gpsimd", "stv0")
                            P.wait("gpsimd", "stv1")
                            P.emit("gpsimd", lambda e: e.collective_compute(
                                "AllGather", ALU.bypass,
                                replica_groups=[[0, 1, 2, 3], [4, 5, 6, 7]],
                                ins=[vstage], outs=[vall]), "cc", 1)
                    P.barrier()

        if upto in ("B", "C"):
            with contextlib.ExitStack() as bes:
                bsb = lambda n, s, d: bes.enter_context(nc.sbuf_tensor(_nm(n), s, d))
                Kb = [bsb("b_K0", [128, 8192], BF16), bsb("b_K1", [128, 8192], BF16)]
                Vb = [bsb("b_V0", [128, 8192], BF16), bsb("b_V1", [128, 8192], BF16)]
                Qb = [bsb("b_Q0", [128, T], BF16), bsb("b_Q1", [128, T], BF16)]
                Ssb = bsb("b_S", [128, 8192], F32)
                Pb = [bsb("b_P0", [128, 8192], BF16), bsb("b_P1", [128, 8192], BF16)]
                PTb = [bsb("b_PT0", [128, 1024], BF16), bsb("b_PT1", [128, 1024], BF16)]
                mxb = bsb("b_mx", [128, 16], F32)
                negm = [bsb("b_negm0", [128, 1], F32), bsb("b_negm1", [128, 1], F32)]
                lrow = [bsb("b_l0", [128, 1], F32), bsb("b_l1", [128, 1], F32)]
                linv = [bsb("b_linv0", [128, 1], F32), bsb("b_linv1", [128, 1], F32)]
                obuf = bsb("b_o", [128, 128], BF16)
                aob = bsb("b_ao", [128, T], BF16)

                kall_r = kall.rearrange("(c4 hh p) (c16 w) -> hh p c16 c4 w",
                                        c4=4, hh=16, p=128, c16=16)
                vall_r = vall.rearrange("(c4 c16 p) (hh w) -> hh p c16 c4 w",
                                        c4=4, c16=16, p=128, hh=16)

                with contextlib.ExitStack() as pes:
                    ps_s = [pes.enter_context(nc.psum_tensor(_nm("ps_sS%d" % i), [128, 1024], F32))
                            for i in range(2)]
                    ps_pt = [pes.enter_context(nc.psum_tensor(_nm("ps_pt%d" % i), [128, 1024], BF16))
                             for i in range(2)]
                    ps_o = pes.enter_context(nc.psum_tensor(_nm("ps_o"), [128, 128], F32))
                    ps_ao = pes.enter_context(nc.psum_tensor(_nm("ps_ao"), [128, 1024], BF16))

                    P.wait("gpsimd", "cc", 2)
                    P.wait("gpsimd", "stq0")
                    P.wait("gpsimd", "stq1")
                    head_rel = [0, 0]
                    srel = {0: [], 1: []}
                    pt_rel = {0: None, 1: None}
                    o_rel_c = [0]
                    ao_evac_c = [0]
                    recip_c = [0, 0]
                    aob_rel = 0
                    for h in range(NH):
                        par = h % 2
                        if head_rel[par]:
                            P.wait("gpsimd", "t", head_rel[par])
                        kvsem = "kv%d" % par
                        for c4 in range(4):
                            P.emit("gpsimd", lambda e, src=kall_r[h, :, :, c4, :],
                                   b=Kb[par], c40=c4: e.dma_start(
                                       out=b.ap().rearrange("p (c16 c4 w) -> p c16 c4 w",
                                                            c16=16, c4=4, w=128)[:, :, c40, :],
                                       in_=src), kvsem, 16)
                        for c4 in range(4):
                            P.emit("gpsimd", lambda e, src=vall_r[h, :, :, c4, :],
                                   b=Vb[par], c40=c4: e.dma_start(
                                       out=b.ap().rearrange("p (c16 c4 w) -> p c16 c4 w",
                                                            c16=16, c4=4, w=128)[:, :, c40, :],
                                       in_=src), kvsem, 16)
                        lq = P.emit("gpsimd", lambda e, b=Qb[par], h0=h:
                                    e.dma_start(out=b[:],
                                                in_=qstage[h0 * 128:(h0 + 1) * 128, :]),
                                    kvsem, 16)
                        P.wait("tensor", kvsem, lq)

                        state = {}

                        def emit_sphase(t0):
                            tp = t0 % 2
                            nb = t0 + 1
                            npair = (nb + 1) // 2
                            for j in range(npair):
                                sp = j % 2
                                blocks = [2 * j] + ([2 * j + 1] if 2 * j + 1 < nb else [])
                                for en, vv in srel[sp]:
                                    P.wait("tensor", en, vv)
                                for bi, b0 in enumerate(blocks):
                                    P.emit("tensor",
                                           lambda e, b1=b0, c0=bi * 512, t1=t0, b=Kb[par],
                                           q=Qb[par], p=ps_s[sp]:
                                           e.matmul(p[:, c0:c0 + 512],
                                                    q[:, t1 * 128:(t1 + 1) * 128],
                                                    b[:, b1 * 512:(b1 + 1) * 512],
                                                    start=True, stop=True),
                                           "t", 1 if bi == len(blocks) - 1 else None)
                                mm_t = P.cnt["t"]
                                rel = []
                                plain = [b for b in blocks if b != t0]
                                if t0 in blocks:
                                    ci = (t0 - 2 * j) * 512
                                    P.wait("vector", "t", mm_t)
                                    ev = P.emit("vector", lambda e, ci0=ci, p=ps_s[sp], t1=t0:
                                                e.scalar_tensor_tensor(
                                                    Ssb[:, t1 * 512:(t1 + 1) * 512],
                                                    p[:, ci0:ci0 + 512], 1.0, mask_s[:],
                                                    ALU.mult, ALU.add), "v", 1)
                                    rel.append(("v", ev))
                                if plain:
                                    c0 = (plain[0] - 2 * j) * 512
                                    w_ = 512 * len(plain)
                                    if j % 2 == 0:
                                        P.wait("scalar", "t", mm_t)
                                        ev = P.emit("scalar",
                                                    lambda e, c1=c0, w1=w_, p=ps_s[sp], bb=plain[0]:
                                                    e.activation(Ssb[:, bb * 512: bb * 512 + w1],
                                                                 p[:, c1:c1 + w1], AF.Copy),
                                                    "s", 1)
                                        rel.append(("s", ev))
                                    else:
                                        P.wait("vector", "t", mm_t)
                                        ev = P.emit("vector",
                                                    lambda e, c1=c0, w1=w_, p=ps_s[sp], bb=plain[0]:
                                                    e.tensor_copy(Ssb[:, bb * 512: bb * 512 + w1],
                                                                  p[:, c1:c1 + w1]),
                                                    "v", 1)
                                        rel.append(("v", ev))
                                srel[sp] = rel
                                for en, vv in rel:
                                    if en != "v":
                                        P.wait("vector", en, vv)
                                wpr = 512 * len(blocks)
                                P.emit("vector", lambda e, j0=j, w1=wpr: e.reduce_max(
                                    mxb[:, j0:j0 + 1], Ssb[:, j0 * 1024: j0 * 1024 + w1],
                                    axis=AX.X))
                            nm_v = P.emit("vector", lambda e, n=npair, tp0=tp: e.reduce_max(
                                negm[tp0][:], mxb[:, 0:n], axis=AX.X, negate=True), "v", 1)
                            P.wait("scalar", "v", nm_v)
                            if state.get("recip%d" % tp):
                                P.wait("scalar", "v", state["recip%d" % tp])
                            if state.get("ptdone%d" % tp):
                                P.wait("scalar", "t", state["ptdone%d" % tp])
                            exp_s = P.emit("scalar", lambda e, kl=512 * nb, tp0=tp:
                                           e.activation(Pb[tp0][:, 0:kl], Ssb[:, 0:kl],
                                                        AF.Exp, bias=negm[tp0][:], scale=1.0,
                                                        accum_out=lrow[tp0][:]), "s", 1)
                            P.wait("vector", "s", exp_s)
                            state["recip%d" % tp] = P.emit(
                                "vector", lambda e, tp0=tp: e.reciprocal(
                                    linv[tp0][:], lrow[tp0][:]), "v", 1)
                            state["exp%d" % tp] = exp_s

                        def emit_ptpv(t0):
                            tp = t0 % 2
                            nkchunks = 4 * (t0 + 1)
                            P.wait("tensor", "s", state["exp%d" % tp])
                            nw = (nkchunks + 7) // 8
                            ptv_of = {}

                            def emit_T(w0):
                                wpar2 = w0 % 2
                                cn = min(8, nkchunks - w0 * 8)
                                if pt_rel[wpar2] is not None:
                                    P.wait("tensor", pt_rel[wpar2][0], pt_rel[wpar2][1])
                                for i in range(cn):
                                    kc = w0 * 8 + i
                                    P.emit("tensor", lambda e, kc0=kc, i0=i, wp=wpar2, tp0=tp:
                                           e.transpose(ps_pt[wp][:, i0 * 128:(i0 + 1) * 128],
                                                       Pb[tp0][:, kc0 * 128:(kc0 + 1) * 128],
                                                       id16[:]),
                                           "t", 1 if i == cn - 1 else None)
                                tw = P.cnt["t"]
                                P.wait("vector", "t", tw)
                                ptv = P.emit("vector", lambda e, wp=wpar2, c=cn:
                                             e.tensor_copy(PTb[wp][:, 0:c * 128],
                                                           ps_pt[wp][:, 0:c * 128]), "v", 1)
                                pt_rel[wpar2] = ("v", ptv)
                                ptv_of[w0] = ("v", ptv)

                            def emit_PV(w0, first):
                                wpar2 = w0 % 2
                                cn = min(8, nkchunks - w0 * 8)
                                P.wait("tensor", ptv_of[w0][0], ptv_of[w0][1])
                                for i in range(cn):
                                    kc = w0 * 8 + i
                                    last = (kc == nkchunks - 1)
                                    P.emit("tensor",
                                           lambda e, kc0=kc, i0=i, wp=wpar2,
                                           st0=(first and i == 0), lst=last, b=Vb[par]:
                                           e.matmul(ps_o[:],
                                                    PTb[wp][:, i0 * 128:(i0 + 1) * 128],
                                                    b[:, kc0 * 128:(kc0 + 1) * 128],
                                                    start=st0, stop=lst),
                                           "t", 1 if last else None)

                            emit_T(0)
                            state["ptdone%d" % tp] = P.cnt["t"]
                            if state.get("o_rel"):
                                P.wait("tensor", "v", state["o_rel"])
                            for w0 in range(nw):
                                if w0 + 1 < nw:
                                    emit_T(w0 + 1)
                                    if w0 + 1 == nw - 1:
                                        state["ptdone%d" % tp] = P.cnt["t"]
                                emit_PV(w0, w0 == 0)
                            pv_t = P.cnt["t"]
                            P.wait("vector", "t", pv_t)
                            state["o_rel"] = P.emit("vector", lambda e, tp0=tp: e.tensor_scalar(
                                obuf[:], ps_o[:], linv[tp0][:], None, ALU.mult), "v", 1)
                            P.wait("tensor", "v", state["o_rel"])
                            if t0 in (0, 8) and state.get("ao_evac"):
                                P.wait("tensor", "v", state["ao_evac"])
                            P.emit("tensor", lambda e, c0=(t0 % 8) * 128: e.transpose(
                                ps_ao[:, c0:c0 + 128], obuf[:], id16[:]), "t", 1)
                            if t0 in (7, 15):
                                tcur = P.cnt["t"]
                                P.wait("vector", "t", tcur)
                                if t0 == 7 and aob_rel:
                                    P.wait("vector", "sta", aob_rel)
                                state["ao_evac"] = P.emit("vector",
                                                          lambda e, c0=(t0 // 8) * 1024:
                                                          e.tensor_copy(aob[:, c0:c0 + 1024],
                                                                        ps_ao[:]), "v", 1)

                        state["o_rel"] = o_rel_c[0]
                        state["ao_evac"] = ao_evac_c[0]
                        state["recip0"] = recip_c[0]
                        state["recip1"] = recip_c[1]
                        for t0 in range(16):
                            emit_sphase(t0)
                            if t0 > 0:
                                emit_ptpv(t0 - 1)
                        emit_ptpv(15)
                        o_rel_c[0] = state["o_rel"]
                        ao_evac_c[0] = state["ao_evac"]
                        recip_c[0] = state["recip0"]
                        recip_c[1] = state["recip1"]
                        head_rel[par] = P.cnt["t"]
                        P.wait("sync", "v", ao_evac_c[0])
                        aob_rel = P.emit("sync", lambda e, h0=h: e.dma_start(
                            out=aostage[h0 * 128:(h0 + 1) * 128, :], in_=aob[:]), "sta", 16)
                P.barrier()

            with contextlib.ExitStack() as prs:
                psb = lambda n, s, d: prs.enter_context(nc.sbuf_tensor(_nm(n), s, d))
                aor = psb("pr_aor", [128, NKC * T], BF16)
                wpb = [psb("pr_wp0", [128, 2048], BF16), psb("pr_wp1", [128, 2048], BF16)]
                x2c = [psb("pr_x2c0", [128, T], F32), psb("pr_x2c1", [128, T], F32)]
                x1c = [psb("pr_x1c0", [128, T], F32), psb("pr_x1c1", [128, T], F32)]
                P.emit("sync", lambda e: e.dma_start(
                    out=aor.ap().rearrange("p (k t) -> p k t", k=NKC),
                    in_=aostage.rearrange("(k p) t -> p k t", p=128)), "qst", 16)
                aor_ld = P.cnt["qst"]
                with contextlib.ExitStack() as pes:
                    ps_pr = [pes.enter_context(nc.psum_tensor(_nm("ps_pr%d" % i), [128, 512], F32))
                             for i in range(2)]
                    P.wait("tensor", "qst", aor_ld)
                    wp_use = [0, 0]
                    pr_rel = {0: 0, 1: 0}
                    x1_ld = [0, 0]
                    x2_rel = [0, 0]
                    x2_st = [0, 0]
                    for d0 in range(16):
                        wpar = d0 % 2
                        sg = "wa%d" % wpar
                        if wp_use[wpar]:
                            P.wait("gpsimd", "t", wp_use[wpar])
                        lw = P.emit("gpsimd", lambda e,
                                    b=wpb[wpar].ap().rearrange("p (k i) -> p k i", k=16),
                                    a=wproj[:, d0 * 128:(d0 + 1) * 128].rearrange(
                                        "(k p) i -> p k i", p=128):
                                    e.dma_start(out=b, in_=a), sg, 16)
                        if x2_rel[wpar]:
                            P.wait("sync", "v", x2_rel[wpar])
                        x1_ld[wpar] = P.emit("sync", lambda e, b=x1c[wpar], d1=d0:
                                             e.dma_start(out=b[:],
                                                         in_=x1d[d1 * 128:(d1 + 1) * 128, :]),
                                             "xl%d" % wpar, 16)
                        P.wait("tensor", sg, lw)
                        if x2_st[wpar]:
                            P.wait("vector", "stp%d" % wpar, x2_st[wpar])
                        for cb in range(4):
                            pp = cb % 2
                            if pr_rel[pp]:
                                P.wait("tensor", "v", pr_rel[pp])
                            for k in range(NKC):
                                P.emit("tensor",
                                       lambda e, k0=k, cb0=cb, b=wpb[wpar], pp0=pp:
                                       e.matmul(ps_pr[pp0][:], b[:, k0 * 128:(k0 + 1) * 128],
                                                aor[:, k0 * T + cb0 * 512: k0 * T + cb0 * 512 + 512],
                                                start=(k0 == 0), stop=(k0 == NKC - 1)),
                                       "t", 1 if k == NKC - 1 else None)
                            mmt = P.cnt["t"]
                            P.wait("vector", "t", mmt)
                            pr_rel[pp] = P.emit("vector",
                                                lambda e, cb0=cb, pp0=pp, d1=d0, wp=wpar:
                                                e.tensor_scalar(x2c[wp][:, cb0 * 512:(cb0 + 1) * 512],
                                                                ps_pr[pp0][:], bproj_s[:, d1:d1 + 1],
                                                                None, ALU.add), "v", 1)
                        wp_use[wpar] = P.cnt["t"]
                        P.wait("vector", "xl%d" % wpar, x1_ld[wpar])
                        vv = P.emit("vector", lambda e, wp=wpar: e.tensor_tensor(
                            x2c[wp][:], x2c[wp][:], x1c[wp][:], ALU.add), "v", 1)
                        x2_rel[wpar] = vv
                        P.wait("sync", "v", vv)
                        x2_st[wpar] = P.emit("sync", lambda e, wp=wpar, d1=d0: e.dma_start(
                            out=x2d[d1 * 128:(d1 + 1) * 128, :], in_=x2c[wp][:]),
                            "stp%d" % wpar, 16)
                P.barrier()

        if upto == "C":
            emit_ffn(lambda k, h: x2d[k * 128:(k + 1) * 128, h * HT:(h + 1) * HT],
                     lambda k, h: outx[k * 128:(k + 1) * 128, h * HT:(h + 1) * HT],
                     "C")

        for nm, (dbg, src) in dump_outs.items():
            P.barrier(["sync"])
            P.emit("sync", lambda e, d=dbg, s0=src: e.dma_start(out=d, in_=s0), "qst", 16)

        for s_ in ("qst", "qgst", "stq0", "stq1", "stv0", "stv1", "sta",
                   "stx0", "stx1", "stp0", "stp1"):
            P.wait("sync", s_)

        block = es.enter_context(nc.Block())

        @block.sync
        def _(e):
            for f_ in P.ops["sync"]:
                f_(e)

        @block.gpsimd
        def _(e):
            for f_ in P.ops["gpsimd"]:
                f_(e)

        @block.tensor
        def _(e):
            for f_ in P.ops["tensor"]:
                f_(e)

        @block.vector
        def _(e):
            for f_ in P.ops["vector"]:
                f_(e)

        @block.scalar
        def _(e):
            for f_ in P.ops["scalar"]:
                f_(e)

    return nc



def _prep_inputs(L0_attn_x, L0_attn_Wqkv, L0_attn_bqkv, L0_attn_Wproj,
                 L0_attn_bproj, L0_ffn_w_gate, L0_ffn_w_up, L0_ffn_w_down):
    x = np.asarray(L0_attn_x, np.float32)
    wqkv_ = np.ascontiguousarray(np.asarray(L0_attn_Wqkv, np.float32))
    wproj_ = np.ascontiguousarray(np.asarray(L0_attn_Wproj, np.float32))
    bqkv_ = np.asarray(L0_attn_bqkv, np.float32)
    bproj_ = np.asarray(L0_attn_bproj, np.float32)
    wgf = np.asarray(L0_ffn_w_gate, np.float32)
    wuf = np.asarray(L0_ffn_w_up, np.float32)
    wdf = np.asarray(L0_ffn_w_down, np.float32)

    wg_p = np.ascontiguousarray(
        wgf.reshape(2048, 16, 128, 4).transpose(0, 1, 3, 2).reshape(2048, 8192))
    wu_p = np.ascontiguousarray(
        wuf.reshape(2048, 16, 128, 4).transpose(0, 1, 3, 2).reshape(2048, 8192))
    wd_p = np.ascontiguousarray(
        wdf.reshape(16, 128, 4, 2048).transpose(0, 2, 1, 3).reshape(8192, 2048))
    bqkv2d = np.ascontiguousarray(bqkv_.reshape(48, 128).T)
    bproj2d = np.ascontiguousarray(bproj_.reshape(16, 128).T)

    in_maps = []
    tok_ids_all = []
    for core in range(8):
        seq, j = core // 4, core % 4
        tok = (np.arange(16)[:, None] * 512 + j * 128 + np.arange(128)[None, :]).reshape(-1)
        tok_ids_all.append((seq, tok))
        x0f = np.ascontiguousarray(x[seq][tok, :].T)
        q = np.arange(128)[:, None]
        w = np.arange(512)[None, :]
        mask = np.where(w <= j * 128 + q, 0.0, NEGBIG).astype(np.float32)
        in_maps.append({
            "x0": x0f, "wqkv": wqkv_, "wproj": wproj_, "wg": wg_p, "wu": wu_p,
            "wd": wd_p, "bqkv": bqkv2d, "bproj": bproj2d, "maskin": mask,
        })
    return in_maps, tok_ids_all


def kernel(L0_attn_x, L0_attn_Wqkv, L0_attn_bqkv, L0_attn_Wproj, L0_attn_bproj,
           L0_ffn_w_gate, L0_ffn_w_up, L0_ffn_w_down):
    in_maps, tok_ids_all = _prep_inputs(
        L0_attn_x, L0_attn_Wqkv, L0_attn_bqkv, L0_attn_Wproj, L0_attn_bproj,
        L0_ffn_w_gate, L0_ffn_w_up, L0_ffn_w_down)
    nc = build_program("C")
    res = run_bass_kernel_spmd(nc, in_maps, core_ids=list(range(8)))
    out = np.zeros((2, 8192, 2048), np.float32)
    for core in range(8):
        seq, tok = tok_ids_all[core]
        out[seq][tok, :] = np.asarray(res.results[core]["outx"], np.float32).T
    return out


# version 09
# speedup vs baseline: 1.0415x; 1.0297x over previous
import contextlib

import numpy as np
import ml_dtypes

import concourse.bass as bass
import concourse.mybir as mybir
from concourse.bass_utils import run_bass_kernel_spmd

F32 = mybir.dt.float32
BF16 = mybir.dt.bfloat16
AF = mybir.ActivationFunctionType
ALU = mybir.AluOpType
AX = mybir.AxisListType

D = 2048
T = 2048
HT = 1024
NI = 8192
NH = 16
NKC = D // 128
EPS = 1e-5
NEGBIG = -1.0e30
QSC = 1.0 / float(np.sqrt(128.0))

ENGS = ("sync", "gpsimd", "tensor", "vector", "scalar")


class Plan:
    def __init__(self, nc, es):
        self.nc = nc
        self.ops = {e: [] for e in ENGS}
        self.sem = {}
        self.cnt = {}
        self.es = es

    def new_sem(self, name):
        self.sem[name] = self.es.enter_context(self.nc.semaphore(name))
        self.cnt[name] = 0

    def wait(self, eng, name, val=None):
        if val is None:
            val = self.cnt[name]
        if val <= 0:
            return
        h = self.sem[name]
        self.ops[eng].append(lambda e, h=h, v=val: e.wait_ge(h, v))

    def emit(self, eng, fn, sem=None, amt=1):
        if sem is None or amt is None:
            self.ops[eng].append(fn)
            return None
        h = self.sem[sem]
        self.cnt[sem] += amt
        self.ops[eng].append(lambda e, fn=fn, h=h, a=amt: fn(e).then_inc(h, a))
        return self.cnt[sem]

    def barrier(self, engs=ENGS):
        snap = dict(self.cnt)
        for eng in engs:
            for name, v in snap.items():
                self.wait(eng, name, v)


def build_program(upto="C", dumps=(), dbg_ffn=False):
    nc = bass.Bass("TRN2", target_bir_lowering=False, debug=False, num_devices=8)

    x0 = nc.dram_tensor("x0", [D, T], F32, kind="ExternalInput").ap()
    wqkv = nc.dram_tensor("wqkv", [D, 3 * D], F32, kind="ExternalInput").ap()
    wproj = nc.dram_tensor("wproj", [D, D], F32, kind="ExternalInput").ap()
    wg = nc.dram_tensor("wg", [D, NI], F32, kind="ExternalInput").ap()
    wu = nc.dram_tensor("wu", [D, NI], F32, kind="ExternalInput").ap()
    wd = nc.dram_tensor("wd", [NI, D], F32, kind="ExternalInput").ap()
    bqkv = nc.dram_tensor("bqkv", [128, 48], F32, kind="ExternalInput").ap()
    bproj = nc.dram_tensor("bproj", [128, 16], F32, kind="ExternalInput").ap()
    maskin = nc.dram_tensor("maskin", [128, 512], F32, kind="ExternalInput").ap()
    outx = nc.dram_tensor("outx", [D, T], F32, kind="ExternalOutput").ap()

    x1d = nc.dram_tensor("x1d", [D, T], F32, kind="Internal").ap()
    x2d = nc.dram_tensor("x2d", [D, T], F32, kind="Internal").ap()
    qstage = nc.dram_tensor("qstage", [D, T], BF16, kind="Internal").ap()
    kstage = nc.dram_tensor("kstage", [D, T], BF16, kind="Internal").ap()
    vstage = nc.dram_tensor("vstage", [T, D], BF16, kind="Internal").ap()
    kall = nc.dram_tensor("kall", [4 * D, T], BF16, kind="Internal").ap()
    vall = nc.dram_tensor("vall", [4 * T, D], BF16, kind="Internal").ap()
    aostage = nc.dram_tensor("aostage", [D, T], BF16, kind="Internal").ap()
    wg16 = nc.dram_tensor("wg16", [64, 128, 2048], BF16, kind="Internal").ap()
    wu16 = nc.dram_tensor("wu16", [64, 128, 2048], BF16, kind="Internal").ap()
    wd16 = nc.dram_tensor("wd16", [64, 128, 2048], BF16, kind="Internal").ap()

    dump_outs = {}
    for nm in dumps:
        src = {"x1": (x1d, F32, [D, T]), "x2": (x2d, F32, [D, T]),
               "kall": (kall, BF16, [4 * D, T]), "vall": (vall, BF16, [4 * T, D]),
               "q": (qstage, BF16, [D, T]), "ao": (aostage, BF16, [D, T]),
               "vstage": (vstage, BF16, [T, D]),
               "wg16": (wg16, BF16, [64, 128, 2048]),
               "wu16": (wu16, BF16, [64, 128, 2048]),
               "wd16": (wd16, BF16, [64, 128, 2048])}[nm]
        dump_outs[nm] = (nc.dram_tensor("dbg_" + nm, src[2], src[1],
                                        kind="ExternalOutput").ap(), src[0])

    if dbg_ffn:
        dbg_xn = nc.dram_tensor("dbg_xn", [128, NKC * HT], BF16, kind="ExternalOutput").ap()
        dbg_rows = nc.dram_tensor("dbg_rows", [2, HT], F32, kind="ExternalOutput").ap()
        dbg_acc = nc.dram_tensor("dbg_acc", [128, NKC * HT], F32, kind="ExternalOutput").ap()
    id_dram = nc.inline_tensor(np.eye(128, dtype=ml_dtypes.bfloat16), "id16d")

    _uid = [0]

    def _nm(n):
        _uid[0] += 1
        return "%s_%d" % (n, _uid[0])

    es = contextlib.ExitStack()
    with es:
        sbp = lambda n, s, d: es.enter_context(nc.sbuf_tensor(_nm(n), s, d))
        id16 = sbp("id16", [128, 128], BF16)
        bqkv_s = sbp("bqkv_s", [128, 48], F32)
        bqs_s = sbp("bqs_s", [128, 16], F32)
        bproj_s = sbp("bproj_s", [128, 16], F32)
        mask_s = sbp("mask_s", [128, 512], F32)
        ones_c = sbp("ones_c", [128, 1], F32)
        ones_r = sbp("ones_r", [1, 128], F32)
        r_a = sbp("r_a", [1, 1024], F32)
        r_b = sbp("r_b", [1, 1024], F32)
        rstd_row = sbp("rstd_row", [1, 1024], F32)
        nm_row = sbp("nm_row", [1, 1024], F32)

        P = Plan(nc, es)
        for s_ in ("qst", "qgst", "xl0", "xl1", "wa0", "wa1", "wb0", "wb1",
                   "wc0", "wc1", "kv0", "kv1", "cc", "t", "v", "s", "g",
                   "stq0", "stq1", "stv0", "stv1", "sta", "stx0", "stx1",
                   "stp0", "stp1"):
            P.new_sem(s_)

        P.emit("sync", lambda e: e.dma_start(out=id16[:], in_=id_dram.ap()), "xl0", 16)
        P.emit("sync", lambda e: e.dma_start(out=bqkv_s[:], in_=bqkv), "xl0", 16)
        P.emit("sync", lambda e: e.dma_start(out=bproj_s[:], in_=bproj), "xl0", 16)
        P.emit("sync", lambda e: e.dma_start(out=mask_s[:], in_=maskin), "xl0", 16)
        P.emit("gpsimd", lambda e: e.memset(ones_c[:], 1.0), "g", 1)
        P.emit("gpsimd", lambda e: e.memset(ones_r[:], 1.0), "g", 1)
        P.wait("vector", "xl0")
        P.emit("vector", lambda e: e.tensor_scalar(
            bqs_s[:], bqkv_s[:, 0:16], QSC, None, ALU.mult), "v", 1)
        P.barrier()


        def emit_stats_half(xsrc_fn, col0, xbufs, sqbufs, W=HT):
            with contextlib.ExitStack() as les:
                ps1 = les.enter_context(nc.psum_tensor(_nm("ps_s1"), [1, W], F32))
                ps2 = les.enter_context(nc.psum_tensor(_nm("ps_s2"), [1, W], F32))
                xb = [b[:, 0:W] for b in xbufs]
                sq = [b[:, 0:W] for b in sqbufs]
                P.wait("sync", "t")
                P.wait("sync", "s")
                P.wait("sync", "v")
                use_t = [0, 0]
                for k in range(NKC):
                    par = k % 2
                    if use_t[par]:
                        P.wait("sync", "t", use_t[par])
                    ld = P.emit("sync", lambda e, a=xsrc_fn(k), b=xb[par]:
                                e.dma_start(out=b, in_=a), "xl%d" % par, 16)
                    P.wait("scalar", "xl%d" % par, ld)
                    if use_t[par]:
                        P.wait("scalar", "t", use_t[par])
                    sc = P.emit("scalar", lambda e, i=xb[par], o=sq[par]:
                                e.activation(o, i, AF.Square), "s", 1)
                    P.wait("tensor", "xl%d" % par, ld)
                    for cb in range(W // 512):
                        P.emit("tensor", lambda e, r=xb[par], p=ps1, c=cb, k0=k:
                               e.matmul(p[:, c * 512:(c + 1) * 512], ones_c[:],
                                        r[:, c * 512:(c + 1) * 512],
                                        start=(k0 == 0), stop=(k0 == NKC - 1)),
                               "t", 1)
                    P.wait("tensor", "s", sc)
                    for cb in range(W // 512):
                        P.emit("tensor", lambda e, r=sq[par], p=ps2, c=cb, k0=k:
                               e.matmul(p[:, c * 512:(c + 1) * 512], ones_c[:],
                                        r[:, c * 512:(c + 1) * 512],
                                        start=(k0 == 0), stop=(k0 == NKC - 1)),
                               "t", 1)
                    use_t[par] = P.cnt["t"]
                P.wait("vector", "t")
                P.emit("vector", lambda e: e.tensor_scalar(
                    r_a[:, 0:W], ps1[:], 1.0 / D, None, ALU.mult), "v", 1)
                P.emit("vector", lambda e: e.tensor_scalar(
                    r_b[:, 0:W], ps2[:], 1.0 / D, None, ALU.mult), "v", 1)
                P.emit("vector", lambda e: e.tensor_tensor(
                    rstd_row[:, col0:col0 + W], r_a[:, 0:W], r_a[:, 0:W], ALU.mult), "v", 1)
                P.emit("vector", lambda e: e.tensor_tensor(
                    r_b[:, 0:W], r_b[:, 0:W], rstd_row[:, col0:col0 + W],
                    ALU.subtract), "v", 1)
                P.emit("vector", lambda e: e.tensor_scalar(
                    r_b[:, 0:W], r_b[:, 0:W], 1.0, EPS, ALU.mult, ALU.add), "v", 1)
                P.wait("scalar", "v")
                P.emit("scalar", lambda e: e.activation(
                    r_b[:, 0:W], r_b[:, 0:W], AF.Sqrt), "s", 1)
                P.wait("vector", "s")
                P.emit("vector", lambda e: e.reciprocal(
                    rstd_row[:, col0:col0 + W], r_b[:, 0:W]), "v", 1)
                P.emit("vector", lambda e: e.scalar_tensor_tensor(
                    nm_row[:, col0:col0 + W], r_a[:, 0:W], -1.0,
                    rstd_row[:, col0:col0 + W], ALU.mult, ALU.mult), "v", 1)
                P.wait("tensor", "v")

        def emit_bcast(col0, W, Rb, Mb):
            with contextlib.ExitStack() as les:
                nb = W // 512
                pbs = [les.enter_context(nc.psum_tensor(_nm("ps_b%d" % i), [128, 512], F32))
                       for i in range(min(nb, 4))]
                P.wait("tensor", "v")
                for i in range(nb):
                    P.emit("tensor", lambda e, i=i, p=pbs[i % len(pbs)]:
                           e.matmul(p[:], ones_r[:],
                                    rstd_row[0:1, col0 + i * 512: col0 + (i + 1) * 512],
                                    start=True, stop=True), "t", 1)
                P.wait("vector", "t")
                for i in range(nb):
                    P.emit("vector", lambda e, i=i, p=pbs[i % len(pbs)]:
                           e.tensor_copy(Rb[:, i * 512:(i + 1) * 512], p[:]), "v", 1)
                P.wait("tensor", "v")
                for i in range(nb):
                    P.emit("tensor", lambda e, i=i, p=pbs[i % len(pbs)]:
                           e.matmul(p[:], ones_r[:],
                                    nm_row[0:1, col0 + i * 512: col0 + (i + 1) * 512],
                                    start=True, stop=True), "t", 1)
                P.wait("vector", "t")
                for i in range(nb):
                    P.emit("vector", lambda e, i=i, p=pbs[i % len(pbs)]:
                           e.tensor_copy(Mb[:, i * 512:(i + 1) * 512], p[:]), "v", 1)
                P.wait("tensor", "v")

        def emit_apply(xsrc_fn, xn, Rb, Mb, W, tmpf, xb, xbsem):
            P.wait("sync", "t")
            P.wait("sync", "s")
            P.wait("sync", "v")
            use_v = [0, 0]
            for k in range(NKC):
                par = k % 2
                if use_v[par]:
                    P.wait("sync", "v", use_v[par])
                ld = P.emit("sync", lambda e, a=xsrc_fn(k), b=xb[par]:
                            e.dma_start(out=b[:], in_=a), xbsem[par], 16)
                P.wait("vector", xbsem[par], ld)
                P.emit("vector", lambda e, b=xb[par]: e.tensor_tensor(
                    tmpf[:, 0:W], b[:, 0:W], Rb[:, 0:W], ALU.mult), "v", 1)
                use_v[par] = P.emit("vector", lambda e, k0=k: e.tensor_tensor(
                    xn[:, k0 * W:(k0 + 1) * W], tmpf[:, 0:W], Mb[:, 0:W],
                    ALU.add), "v", 1)


        def emit_ffn(xin_chunk, xout_chunk, app):
            with contextlib.ExitStack() as fes:
                fsb = lambda n, s, d: fes.enter_context(nc.sbuf_tensor(_nm(n), s, d))
                xn = fsb("ffn_xn", [128, NKC * HT], BF16)
                acc = fsb("ffn_acc", [128, NKC * HT], F32)
                slab = fsb("ffn_slab", [128, 8 * HT], BF16)
                Rb = fsb("ffn_Rb", [128, HT], F32)
                Mb = fsb("ffn_Mb", [128, HT], F32)
                tmpf = fsb("ffn_tmpf", [128, HT], F32)
                xb = [fsb("ffn_xb0", [128, HT], F32), fsb("ffn_xb1", [128, HT], F32)]
                wgb = [fsb("wgb0", [128, 2048], BF16), fsb("wgb1", [128, 2048], BF16)]
                wub = [fsb("wub0", [128, 2048], BF16), fsb("wub1", [128, 2048], BF16)]
                wdb = [fsb("wdb0", [128, 2048], BF16), fsb("wdb1", [128, 2048], BF16)]
                hq = [fsb("hq%d" % r, [128, HT], BF16) for r in range(4)]
                na = [fsb("na%d" % r, [128, HT], BF16) for r in range(4)]
                tA = fsb("tA", [128, HT], BF16)
                tC = fsb("tC", [128, HT], BF16)
                tD = fsb("tD", [128, HT], BF16)
                x1b = [fsb("x1b0", [128, HT], F32), fsb("x1b1", [128, HT], F32)]

                wg_use = [0, 0]
                wd_use = [0, 0]

                for half in range(2):
                    fresh = (app == "A" and half == 0)

                    def wsrc(which, ic):
                        if fresh:
                            if which == "d":
                                g0, dp = ic // 8, ic % 8
                                return wd[g0 * 1024:(g0 + 1) * 1024,
                                          dp * 256:(dp + 1) * 256].rearrange(
                                              "(s p) d -> p s d", p=128)
                            src = wg if which == "g" else wu
                            return src[:, ic * 128:(ic + 1) * 128].rearrange(
                                "(k p) i -> p k i", p=128)
                        cache = {"g": wg16, "u": wu16, "d": wd16}[which]
                        return cache[ic]

                    emit_stats_half(lambda k: xin_chunk(k, half), 0, xb, x1b, HT)
                    if dbg_ffn and app == "A" and half == 0:
                        P.wait("sync", "v")
                        P.emit("sync", lambda e: e.dma_start(
                            out=dbg_rows[0:1, :], in_=rstd_row[:]), "qst", 16)
                        P.emit("sync", lambda e: e.dma_start(
                            out=dbg_rows[1:2, :], in_=nm_row[:]), "qst", 16)
                    emit_bcast(0, HT, Rb, Mb)
                    emit_apply(lambda k: xin_chunk(k, half), xn, Rb, Mb, HT,
                               tmpf, xb, ("xl0", "xl1"))
                    xn_done_v = P.cnt["v"]
                    if dbg_ffn and app == "A" and half == 0:
                        P.wait("sync", "v", xn_done_v)
                        P.emit("sync", lambda e: e.dma_start(
                            out=dbg_xn, in_=xn[:]), "qst", 16)

                    with contextlib.ExitStack() as pes:
                        ps_g = [pes.enter_context(nc.psum_tensor(_nm("ps_g%d" % i), [128, 512], F32))
                                for i in range(2)]
                        ps_u = [pes.enter_context(nc.psum_tensor(_nm("ps_u%d" % i), [128, 512], F32))
                                for i in range(2)]
                        ps_d = [pes.enter_context(nc.psum_tensor(_nm("ps_d%d" % i), [128, 512], F32))
                                for i in range(4)]
                        P.wait("tensor", "v", xn_done_v)
                        gate_rel = {0: 0, 1: 0}
                        up_rel = {0: 0, 1: 0}
                        dn_rel = {0: 0, 1: 0, 2: 0, 3: 0}
                        slab_free_t = 0
                        hq_free_v = 0

                        for g0 in range(8):
                            for ccq in range(2):
                                cc = g0 * 2 + ccq
                                for r in range(4):
                                    ic = 4 * cc + r
                                    wpar = (cc * 4 + r) % 2
                                    sg, su = "wa%d" % wpar, "wb%d" % wpar
                                    gq = "gpsimd" if fresh else "sync"
                                    uq = "gpsimd" if fresh else "scalar"
                                    if wg_use[wpar]:
                                        P.wait(gq, "t", wg_use[wpar])
                                        P.wait(uq, "t", wg_use[wpar])
                                    if fresh:
                                        P.wait("gpsimd", "qgst")
                                    dstg = wgb[wpar].ap().rearrange("p (k i) -> p k i", k=16) \
                                        if fresh else wgb[wpar][:]
                                    lg = P.emit(gq, lambda e, a=wsrc("g", ic), b=dstg:
                                                e.dma_start(out=b, in_=a), sg, 16)
                                    if fresh:
                                        P.wait("scalar", sg, lg)
                                        P.emit("scalar", lambda e, b=wgb[wpar], o=wg16[ic]:
                                               e.dma_start(out=o, in_=b[:]), "qgst", 16)
                                    dstu = wub[wpar].ap().rearrange("p (k i) -> p k i", k=16) \
                                        if fresh else wub[wpar][:]
                                    lu = P.emit(uq, lambda e, a=wsrc("u", ic), b=dstu:
                                                e.dma_start(out=b, in_=a), su, 16)
                                    if fresh:
                                        P.wait("scalar", su, lu)
                                        P.emit("scalar", lambda e, b=wub[wpar], o=wu16[ic]:
                                               e.dma_start(out=o, in_=b[:]), "qgst", 16)
                                    P.wait("tensor", sg, lg)
                                    P.wait("tensor", su, lu)
                                    for tb in range(2):
                                        if gate_rel[tb]:
                                            P.wait("tensor", "s", gate_rel[tb])
                                        for k in range(NKC):
                                            P.emit("tensor",
                                                   lambda e, k0=k, pg=ps_g[tb], b=wgb[wpar], tb0=tb:
                                                   e.matmul(pg[:], b[:, k0 * 128:(k0 + 1) * 128],
                                                            xn[:, k0 * HT + tb0 * 512: k0 * HT + tb0 * 512 + 512],
                                                            start=(k0 == 0), stop=(k0 == NKC - 1)),
                                                   "t", 1 if k == NKC - 1 else None)
                                        if up_rel[tb]:
                                            P.wait("tensor", "v", up_rel[tb])
                                        for k in range(NKC):
                                            P.emit("tensor",
                                                   lambda e, k0=k, pu=ps_u[tb], b=wub[wpar], tb0=tb:
                                                   e.matmul(pu[:], b[:, k0 * 128:(k0 + 1) * 128],
                                                            xn[:, k0 * HT + tb0 * 512: k0 * HT + tb0 * 512 + 512],
                                                            start=(k0 == 0), stop=(k0 == NKC - 1)),
                                                   "t", 1 if k == NKC - 1 else None)
                                    wg_use[wpar] = P.cnt["t"]
                                    tcnt = P.cnt["t"]
                                    gs = {0: tcnt - 3, 1: tcnt - 1}
                                    us = {0: tcnt - 2, 1: tcnt}
                                    for tb in range(2):
                                        P.wait("scalar", "t", gs[tb])
                                        if hq_free_v:
                                            P.wait("scalar", "v", hq_free_v)
                                        sc = P.emit("scalar", lambda e, pg=ps_g[tb], r0=r, tb0=tb:
                                                    e.activation(hq[r0][:, tb0 * 512: tb0 * 512 + 512],
                                                                 pg[:], AF.Silu), "s", 1)
                                        gate_rel[tb] = sc
                                        P.wait("vector", "s", sc)
                                        P.wait("vector", "t", us[tb])
                                        mv = P.emit("vector", lambda e, pu=ps_u[tb], r0=r, tb0=tb:
                                                    e.tensor_tensor(
                                                        hq[r0][:, tb0 * 512: tb0 * 512 + 512],
                                                        hq[r0][:, tb0 * 512: tb0 * 512 + 512],
                                                        pu[:], ALU.mult), "v", 1)
                                        up_rel[tb] = mv
                                if slab_free_t:
                                    P.wait("vector", "t", slab_free_t)
                                for r in range(4):
                                    P.emit("vector", lambda e, r0=r: e.scalar_tensor_tensor(
                                        na[r0][:], hq[r0][:], -1.0, hq[r0][:],
                                        ALU.mult, ALU.min))
                                P.emit("vector", lambda e: e.tensor_tensor(tA[:], na[0][:], na[1][:], ALU.min))
                                P.emit("vector", lambda e: e.tensor_tensor(tC[:], na[2][:], na[3][:], ALU.min))
                                P.emit("vector", lambda e: e.tensor_tensor(tA[:], tA[:], tC[:], ALU.max))
                                P.emit("vector", lambda e: e.tensor_tensor(tC[:], na[0][:], na[1][:], ALU.max))
                                P.emit("vector", lambda e: e.tensor_tensor(tD[:], na[2][:], na[3][:], ALU.max))
                                P.emit("vector", lambda e: e.tensor_tensor(tC[:], tC[:], tD[:], ALU.min))
                                P.emit("vector", lambda e: e.tensor_tensor(tA[:], tA[:], tC[:], ALU.min))
                                for r in range(4):
                                    P.emit("vector", lambda e, r0=r: e.tensor_tensor(
                                        tD[:], na[r0][:], tA[:], ALU.is_le))
                                    slot = (cc - 2 * g0) * 4 + r
                                    hq_free_v = P.emit("vector", lambda e, r0=r, s0=slot:
                                                       e.tensor_tensor(
                                                           slab[:, s0 * HT:(s0 + 1) * HT],
                                                           hq[r0][:], tD[:], ALU.mult), "v", 1)
                            net_done_v = P.cnt["v"]
                            P.wait("tensor", "v", net_done_v)
                            for dp in range(8):
                                wpar = dp % 2
                                sd = "wc%d" % wpar
                                dq = "gpsimd" if fresh else ("sync" if wpar == 0 else "scalar")
                                if wd_use[wpar]:
                                    P.wait(dq, "t", wd_use[wpar])
                                if fresh:
                                    P.wait("gpsimd", "qgst")
                                dstd = wdb[wpar].ap().rearrange("p (s d) -> p s d", s=8) \
                                    if fresh else wdb[wpar][:]
                                ldd = P.emit(dq, lambda e, a=wsrc("d", g0 * 8 + dp), b=dstd:
                                             e.dma_start(out=b, in_=a), sd, 16)
                                if fresh:
                                    P.wait("scalar", sd, ldd)
                                    P.emit("scalar", lambda e, b=wdb[wpar], o=wd16[g0 * 8 + dp]:
                                           e.dma_start(out=o, in_=b[:]), "qgst", 16)
                                P.wait("tensor", sd, ldd)
                                for dd in range(2):
                                    d0 = 2 * dp + dd
                                    for tb in range(2):
                                        dpi = dd * 2 + tb
                                        if dn_rel[dpi]:
                                            P.wait("tensor", "v", dn_rel[dpi])
                                        for s0 in range(8):
                                            P.emit("tensor",
                                                   lambda e, s1=s0, pd=ps_d[dpi], b=wdb[wpar],
                                                   tb0=tb, dd0=dd:
                                                   e.matmul(pd[:], b[:, s1 * 256 + dd0 * 128: s1 * 256 + dd0 * 128 + 128],
                                                            slab[:, s1 * HT + tb0 * 512: s1 * HT + tb0 * 512 + 512],
                                                            start=(s1 == 0), stop=(s1 == 7)),
                                                   "t", 1 if s0 == 7 else None)
                                        mmc = P.cnt["t"]
                                        P.wait("vector", "t", mmc)
                                        if g0 == 0:
                                            dn_rel[dpi] = P.emit(
                                                "vector", lambda e, pd=ps_d[dpi], d1=d0, tb0=tb:
                                                e.tensor_copy(
                                                    acc[:, d1 * HT + tb0 * 512: d1 * HT + tb0 * 512 + 512],
                                                    pd[:]), "v", 1)
                                        else:
                                            dn_rel[dpi] = P.emit(
                                                "vector", lambda e, pd=ps_d[dpi], d1=d0, tb0=tb:
                                                e.tensor_tensor(
                                                    acc[:, d1 * HT + tb0 * 512: d1 * HT + tb0 * 512 + 512],
                                                    acc[:, d1 * HT + tb0 * 512: d1 * HT + tb0 * 512 + 512],
                                                    pd[:], ALU.add), "v", 1)
                                wd_use[wpar] = P.cnt["t"]
                            slab_free_t = P.cnt["t"]

                    if dbg_ffn and app == "A" and half == 0:
                        P.wait("sync", "v")
                        P.emit("sync", lambda e: e.dma_start(
                            out=dbg_acc, in_=acc[:]), "qst", 16)
                        P.wait("sync", "qst")
                    P.wait("sync", "t")
                    P.wait("sync", "s")
                    P.wait("sync", "v")
                    usef = [0, 0]
                    stcnt = [0, 0]
                    for k in range(NKC):
                        par = k % 2
                        if usef[par]:
                            P.wait("sync", "v", usef[par])
                        ldf = P.emit("sync", lambda e, a=xin_chunk(k, half), b=xb[par]:
                                     e.dma_start(out=b[:], in_=a), "xl%d" % par, 16)
                        P.wait("vector", "xl%d" % par, ldf)
                        if stcnt[par]:
                            P.wait("vector", "stx%d" % par, stcnt[par])
                        usef[par] = P.emit("vector", lambda e, k0=k, b=xb[par], o=x1b[par]:
                                           e.scalar_tensor_tensor(
                                               o[:], acc[:, k0 * HT:(k0 + 1) * HT], 0.5,
                                               b[:], ALU.mult, ALU.add), "v", 1)
                        P.wait("sync", "v", usef[par])
                        stcnt[par] = P.emit("sync", lambda e, o=x1b[par], dst=xout_chunk(k, half):
                                            e.dma_start(out=dst, in_=o[:]), "stx%d" % par, 16)
                    P.barrier()


        if upto in ("A", "A2", "B", "C"):
            emit_ffn(lambda k, h: x0[k * 128:(k + 1) * 128, h * HT:(h + 1) * HT],
                     lambda k, h: x1d[k * 128:(k + 1) * 128, h * HT:(h + 1) * HT],
                     "A")

        if upto in ("A2", "B", "C"):
            with contextlib.ExitStack() as aes:
                asb = lambda n, s, d: aes.enter_context(nc.sbuf_tensor(_nm(n), s, d))
                Rb2 = asb("a2_Rb", [128, T], F32)
                Mb2 = asb("a2_Mb", [128, T], F32)
                tmp2 = asb("a2_tmp", [128, T], F32)
                xb2 = [asb("a2_xb0", [128, T], F32), asb("a2_xb1", [128, T], F32)]
                a2sq = asb("a2_sq", [128, HT], F32)
                xn2 = asb("a2_xn", [128, NKC * T], BF16)
                for half in range(2):
                    emit_stats_half(
                        lambda k, h=half: x1d[k * 128:(k + 1) * 128, h * HT:(h + 1) * HT],
                        0, xb2, [tmp2, a2sq], HT)
                    emit_bcast(0, HT, Rb2[:, half * HT:(half + 1) * HT],
                               Mb2[:, half * HT:(half + 1) * HT])
                emit_apply(lambda k: x1d[k * 128:(k + 1) * 128, :], xn2, Rb2, Mb2, T,
                           tmp2, xb2, ("xl0", "xl1"))
                xn2_done = P.cnt["v"]
                qsg = [asb("a2_qs0", [128, T], BF16), asb("a2_qs1", [128, T], BF16)]
                vtb = [asb("a2_vt0", [128, 1024], BF16), asb("a2_vt1", [128, 1024], BF16)]
                wqb = [asb("a2_wq0", [128, 2048], BF16), asb("a2_wq1", [128, 2048], BF16)]
                with contextlib.ExitStack() as pes:
                    ps_q = [pes.enter_context(nc.psum_tensor(_nm("ps_q%d" % i), [128, 1024], F32))
                            for i in range(3)]
                    ps_vt = [pes.enter_context(nc.psum_tensor(_nm("ps_vt%d" % i), [128, 1024], BF16))
                             for i in range(2)]
                    P.wait("tensor", "v", xn2_done)
                    wq_use = [0, 0]
                    q_rel = {0: 0, 1: 0, 2: 0}
                    qslot = [0]
                    stg_rel = [0, 0]
                    stg_rel_t = [0, 0]
                    vt_rel = {0: 0, 1: 0}
                    vtb_rel = [0, 0]
                    forder = list(range(16, 48)) + list(range(16))
                    for fi, f in enumerate(forder):
                        wpar = fi % 2
                        sg = "wa%d" % wpar
                        if wq_use[wpar]:
                            P.wait("gpsimd", "t", wq_use[wpar])
                        lq = P.emit("gpsimd", lambda e,
                                    b=wqb[wpar].ap().rearrange("p (k i) -> p k i", k=16),
                                    a=wqkv[:, f * 128:(f + 1) * 128].rearrange(
                                        "(k p) i -> p k i", p=128):
                                    e.dma_start(out=b, in_=a), sg, 16)
                        P.wait("tensor", sg, lq)
                        slots = []
                        for hb in range(2):
                            sl = qslot[0]
                            qslot[0] = (qslot[0] + 1) % 3
                            slots.append(sl)
                            if q_rel[sl]:
                                P.wait("tensor", "v", q_rel[sl])
                            for cb in range(2):
                                for k in range(NKC):
                                    P.emit("tensor",
                                           lambda e, k0=k, hb0=hb, cb0=cb, b=wqb[wpar], sl0=sl:
                                           e.matmul(ps_q[sl0][:, cb0 * 512:(cb0 + 1) * 512],
                                                    b[:, k0 * 128:(k0 + 1) * 128],
                                                    xn2[:, k0 * T + hb0 * 1024 + cb0 * 512:
                                                        k0 * T + hb0 * 1024 + cb0 * 512 + 512],
                                                    start=(k0 == 0), stop=(k0 == NKC - 1)),
                                           "t", 1 if (k == NKC - 1 and cb == 1) else None)
                        wq_use[wpar] = P.cnt["t"]
                        tq = P.cnt["t"]
                        spar = fi % 2
                        if stg_rel[spar]:
                            P.wait("vector", "stq%d" % spar, stg_rel[spar])
                        if stg_rel_t[spar]:
                            P.wait("vector", "t", stg_rel_t[spar])
                        bias_ap = bqkv_s[:, f:f + 1]
                        scl = QSC if f < 16 else None
                        for hb in range(2):
                            sl = slots[hb]
                            P.wait("vector", "t", tq - 1 + hb)
                            q_rel[sl] = P.emit("vector",
                                               lambda e, hb0=hb, b=bias_ap, sc0=scl, sp=spar, sl0=sl:
                                               e.tensor_scalar(qsg[sp][:, hb0 * 1024:(hb0 + 1) * 1024],
                                                               ps_q[sl0][:], b, sc0, ALU.add,
                                                               ALU.mult) if sc0 is not None else
                                               e.tensor_scalar(qsg[sp][:, hb0 * 1024:(hb0 + 1) * 1024],
                                                               ps_q[sl0][:], b, None, ALU.add),
                                               "v", 1)
                        se = P.cnt["v"]
                        if f < 32:
                            dst = (qstage if f < 16 else kstage)
                            row = (f % 16) * 128
                            P.wait("sync", "v", se)
                            stg_rel[spar] = P.emit("sync", lambda e, sp=spar, d=dst, r0=row:
                                                   e.dma_start(out=d[r0:r0 + 128, :],
                                                               in_=qsg[sp][:]),
                                                   "stq%d" % spar, 16)
                        else:
                            h0 = f - 32
                            P.wait("tensor", "v", se)
                            for w0 in range(2):
                                if vt_rel[w0]:
                                    P.wait("tensor", vt_rel[w0][0], vt_rel[w0][1])
                                for tc in range(8):
                                    tcg = w0 * 8 + tc
                                    P.emit("tensor", lambda e, sp=spar, t0=tcg, w1=w0, tc0=tc:
                                           e.transpose(ps_vt[w1][:, tc0 * 128:(tc0 + 1) * 128],
                                                       qsg[sp][:, t0 * 128:(t0 + 1) * 128],
                                                       id16[:]),
                                           "t", 1 if tc == 7 else None)
                                tw = P.cnt["t"]
                                P.wait("scalar", "t", tw)
                                if vtb_rel[w0]:
                                    P.wait("scalar", "stv%d" % w0, vtb_rel[w0])
                                vt_rel_s = P.emit("scalar", lambda e, w1=w0:
                                                  e.activation(vtb[w1][:], ps_vt[w1][:],
                                                               AF.Copy), "s", 1)
                                vt_rel[w0] = ("s", vt_rel_s)
                                P.wait("sync", "s", vt_rel_s)
                                dstv = vstage[w0 * 1024: w0 * 1024 + 1024,
                                              h0 * 128:(h0 + 1) * 128].rearrange(
                                                  "(tc p) dh -> p tc dh", p=128)
                                srcv = vtb[w0].ap().rearrange("p (tc dh) -> p tc dh", tc=8)
                                vtb_rel[w0] = P.emit("sync", lambda e, dv=dstv, sv=srcv:
                                                     e.dma_start(out=dv, in_=sv),
                                                     "stv%d" % w0, 16)
                            stg_rel[spar] = P.cnt["stq%d" % spar]
                            stg_rel_t[spar] = P.cnt["t"]
                        if f == 31:
                            P.wait("gpsimd", "stq0")
                            P.wait("gpsimd", "stq1")
                            P.emit("gpsimd", lambda e: e.collective_compute(
                                "AllGather", ALU.bypass,
                                replica_groups=[[0, 1, 2, 3], [4, 5, 6, 7]],
                                ins=[kstage], outs=[kall]), "cc", 1)
                        if f == 47:
                            P.wait("gpsimd", "stv0")
                            P.wait("gpsimd", "stv1")
                            P.emit("gpsimd", lambda e: e.collective_compute(
                                "AllGather", ALU.bypass,
                                replica_groups=[[0, 1, 2, 3], [4, 5, 6, 7]],
                                ins=[vstage], outs=[vall]), "cc", 1)
                    P.barrier()

        if upto in ("B", "C"):
            with contextlib.ExitStack() as bes:
                bsb = lambda n, s, d: bes.enter_context(nc.sbuf_tensor(_nm(n), s, d))
                Kb = [bsb("b_K0", [128, 8192], BF16), bsb("b_K1", [128, 8192], BF16)]
                Vb = [bsb("b_V0", [128, 8192], BF16), bsb("b_V1", [128, 8192], BF16)]
                Qb = [bsb("b_Q0", [128, T], BF16), bsb("b_Q1", [128, T], BF16)]
                Ssb = bsb("b_S", [128, 8192], F32)
                Pb = [bsb("b_P0", [128, 8192], BF16), bsb("b_P1", [128, 8192], BF16)]
                PTb = [bsb("b_PT0", [128, 1024], BF16), bsb("b_PT1", [128, 1024], BF16)]
                mxb = bsb("b_mx", [128, 16], F32)
                negm = [bsb("b_negm0", [128, 1], F32), bsb("b_negm1", [128, 1], F32)]
                lrow = [bsb("b_l0", [128, 1], F32), bsb("b_l1", [128, 1], F32)]
                linv = [bsb("b_linv0", [128, 1], F32), bsb("b_linv1", [128, 1], F32)]
                obuf = bsb("b_o", [128, 128], BF16)
                aob = bsb("b_ao", [128, T], BF16)

                kall_r = kall.rearrange("(c4 hh p) (c16 w) -> hh p c16 c4 w",
                                        c4=4, hh=16, p=128, c16=16)
                vall_r = vall.rearrange("(c4 c16 p) (hh w) -> hh p c16 c4 w",
                                        c4=4, c16=16, p=128, hh=16)

                with contextlib.ExitStack() as pes:
                    ps_s = [pes.enter_context(nc.psum_tensor(_nm("ps_sS%d" % i), [128, 1024], F32))
                            for i in range(2)]
                    ps_pt = [pes.enter_context(nc.psum_tensor(_nm("ps_pt%d" % i), [128, 1024], BF16))
                             for i in range(2)]
                    ps_o = pes.enter_context(nc.psum_tensor(_nm("ps_o"), [128, 128], F32))
                    ps_ao = pes.enter_context(nc.psum_tensor(_nm("ps_ao"), [128, 1024], BF16))

                    P.wait("gpsimd", "cc", 2)
                    P.wait("gpsimd", "stq0")
                    P.wait("gpsimd", "stq1")
                    head_rel = [0, 0]
                    srel = {0: [], 1: []}
                    pt_rel = {0: None, 1: None}
                    o_rel_c = [0]
                    ao_evac_c = [0]
                    recip_c = [0, 0]
                    aob_rel = 0
                    for h in range(NH):
                        par = h % 2
                        if head_rel[par]:
                            P.wait("gpsimd", "t", head_rel[par])
                        kvsem = "kv%d" % par
                        for c4 in range(4):
                            P.emit("gpsimd", lambda e, src=kall_r[h, :, :, c4, :],
                                   b=Kb[par], c40=c4: e.dma_start(
                                       out=b.ap().rearrange("p (c16 c4 w) -> p c16 c4 w",
                                                            c16=16, c4=4, w=128)[:, :, c40, :],
                                       in_=src), kvsem, 16)
                        for c4 in range(4):
                            P.emit("gpsimd", lambda e, src=vall_r[h, :, :, c4, :],
                                   b=Vb[par], c40=c4: e.dma_start(
                                       out=b.ap().rearrange("p (c16 c4 w) -> p c16 c4 w",
                                                            c16=16, c4=4, w=128)[:, :, c40, :],
                                       in_=src), kvsem, 16)
                        lq = P.emit("gpsimd", lambda e, b=Qb[par], h0=h:
                                    e.dma_start(out=b[:],
                                                in_=qstage[h0 * 128:(h0 + 1) * 128, :]),
                                    kvsem, 16)
                        P.wait("tensor", kvsem, lq)

                        state = {}

                        def emit_sphase(t0):
                            tp = t0 % 2
                            nb = t0 + 1
                            npair = (nb + 1) // 2
                            for j in range(npair):
                                sp = j % 2
                                blocks = [2 * j] + ([2 * j + 1] if 2 * j + 1 < nb else [])
                                for en, vv in srel[sp]:
                                    P.wait("tensor", en, vv)
                                for bi, b0 in enumerate(blocks):
                                    P.emit("tensor",
                                           lambda e, b1=b0, c0=bi * 512, t1=t0, b=Kb[par],
                                           q=Qb[par], p=ps_s[sp]:
                                           e.matmul(p[:, c0:c0 + 512],
                                                    q[:, t1 * 128:(t1 + 1) * 128],
                                                    b[:, b1 * 512:(b1 + 1) * 512],
                                                    start=True, stop=True),
                                           "t", 1 if bi == len(blocks) - 1 else None)
                                mm_t = P.cnt["t"]
                                rel = []
                                plain = [b for b in blocks if b != t0]
                                if t0 in blocks:
                                    ci = (t0 - 2 * j) * 512
                                    P.wait("vector", "t", mm_t)
                                    ev = P.emit("vector", lambda e, ci0=ci, p=ps_s[sp], t1=t0:
                                                e.scalar_tensor_tensor(
                                                    Ssb[:, t1 * 512:(t1 + 1) * 512],
                                                    p[:, ci0:ci0 + 512], 1.0, mask_s[:],
                                                    ALU.mult, ALU.add), "v", 1)
                                    rel.append(("v", ev))
                                if plain:
                                    c0 = (plain[0] - 2 * j) * 512
                                    w_ = 512 * len(plain)
                                    P.wait("scalar", "t", mm_t)
                                    ev = P.emit("scalar",
                                                lambda e, c1=c0, w1=w_, p=ps_s[sp], bb=plain[0]:
                                                e.activation(Ssb[:, bb * 512: bb * 512 + w1],
                                                             p[:, c1:c1 + w1], AF.Copy),
                                                "s", 1)
                                    rel.append(("s", ev))
                                srel[sp] = rel
                                for en, vv in rel:
                                    if en != "v":
                                        P.wait("vector", en, vv)
                                wpr = 512 * len(blocks)
                                P.emit("vector", lambda e, j0=j, w1=wpr: e.reduce_max(
                                    mxb[:, j0:j0 + 1], Ssb[:, j0 * 1024: j0 * 1024 + w1],
                                    axis=AX.X))
                            nm_v = P.emit("vector", lambda e, n=npair, tp0=tp: e.reduce_max(
                                negm[tp0][:], mxb[:, 0:n], axis=AX.X, negate=True), "v", 1)
                            P.wait("scalar", "v", nm_v)
                            if state.get("recip%d" % tp):
                                P.wait("scalar", "v", state["recip%d" % tp])
                            if state.get("ptdone%d" % tp):
                                P.wait("scalar", "t", state["ptdone%d" % tp])
                            exp_s = P.emit("scalar", lambda e, kl=512 * nb, tp0=tp:
                                           e.activation(Pb[tp0][:, 0:kl], Ssb[:, 0:kl],
                                                        AF.Exp, bias=negm[tp0][:], scale=1.0,
                                                        accum_out=lrow[tp0][:]), "s", 1)
                            P.wait("vector", "s", exp_s)
                            state["recip%d" % tp] = P.emit(
                                "vector", lambda e, tp0=tp: e.reciprocal(
                                    linv[tp0][:], lrow[tp0][:]), "v", 1)
                            state["exp%d" % tp] = exp_s

                        def emit_ptpv(t0):
                            tp = t0 % 2
                            nkchunks = 4 * (t0 + 1)
                            P.wait("tensor", "s", state["exp%d" % tp])
                            nw = (nkchunks + 7) // 8
                            ptv_of = {}

                            def emit_T(w0):
                                wpar2 = w0 % 2
                                cn = min(8, nkchunks - w0 * 8)
                                if pt_rel[wpar2] is not None:
                                    P.wait("tensor", pt_rel[wpar2][0], pt_rel[wpar2][1])
                                for i in range(cn):
                                    kc = w0 * 8 + i
                                    P.emit("tensor", lambda e, kc0=kc, i0=i, wp=wpar2, tp0=tp:
                                           e.transpose(ps_pt[wp][:, i0 * 128:(i0 + 1) * 128],
                                                       Pb[tp0][:, kc0 * 128:(kc0 + 1) * 128],
                                                       id16[:]),
                                           "t", 1 if i == cn - 1 else None)
                                tw = P.cnt["t"]
                                P.wait("vector", "t", tw)
                                ptv = P.emit("vector", lambda e, wp=wpar2, c=cn:
                                             e.tensor_copy(PTb[wp][:, 0:c * 128],
                                                           ps_pt[wp][:, 0:c * 128]), "v", 1)
                                pt_rel[wpar2] = ("v", ptv)
                                ptv_of[w0] = ("v", ptv)

                            def emit_PV(w0, first):
                                wpar2 = w0 % 2
                                cn = min(8, nkchunks - w0 * 8)
                                P.wait("tensor", ptv_of[w0][0], ptv_of[w0][1])
                                for i in range(cn):
                                    kc = w0 * 8 + i
                                    last = (kc == nkchunks - 1)
                                    P.emit("tensor",
                                           lambda e, kc0=kc, i0=i, wp=wpar2,
                                           st0=(first and i == 0), lst=last, b=Vb[par]:
                                           e.matmul(ps_o[:],
                                                    PTb[wp][:, i0 * 128:(i0 + 1) * 128],
                                                    b[:, kc0 * 128:(kc0 + 1) * 128],
                                                    start=st0, stop=lst),
                                           "t", 1 if last else None)

                            emit_T(0)
                            state["ptdone%d" % tp] = P.cnt["t"]
                            if state.get("o_rel"):
                                P.wait("tensor", "v", state["o_rel"])
                            for w0 in range(nw):
                                if w0 + 1 < nw:
                                    emit_T(w0 + 1)
                                    if w0 + 1 == nw - 1:
                                        state["ptdone%d" % tp] = P.cnt["t"]
                                emit_PV(w0, w0 == 0)
                            pv_t = P.cnt["t"]
                            P.wait("vector", "t", pv_t)
                            state["o_rel"] = P.emit("vector", lambda e, tp0=tp: e.tensor_scalar(
                                obuf[:], ps_o[:], linv[tp0][:], None, ALU.mult), "v", 1)
                            P.wait("tensor", "v", state["o_rel"])
                            if t0 in (0, 8) and state.get("ao_evac"):
                                P.wait("tensor", "v", state["ao_evac"])
                            P.emit("tensor", lambda e, c0=(t0 % 8) * 128: e.transpose(
                                ps_ao[:, c0:c0 + 128], obuf[:], id16[:]), "t", 1)
                            if t0 in (7, 15):
                                tcur = P.cnt["t"]
                                P.wait("vector", "t", tcur)
                                if t0 == 7 and aob_rel:
                                    P.wait("vector", "sta", aob_rel)
                                state["ao_evac"] = P.emit("vector",
                                                          lambda e, c0=(t0 // 8) * 1024:
                                                          e.tensor_copy(aob[:, c0:c0 + 1024],
                                                                        ps_ao[:]), "v", 1)

                        state["o_rel"] = o_rel_c[0]
                        state["ao_evac"] = ao_evac_c[0]
                        state["recip0"] = recip_c[0]
                        state["recip1"] = recip_c[1]
                        for t0 in range(16):
                            emit_sphase(t0)
                            if t0 > 0:
                                emit_ptpv(t0 - 1)
                        emit_ptpv(15)
                        o_rel_c[0] = state["o_rel"]
                        ao_evac_c[0] = state["ao_evac"]
                        recip_c[0] = state["recip0"]
                        recip_c[1] = state["recip1"]
                        head_rel[par] = P.cnt["t"]
                        P.wait("sync", "v", ao_evac_c[0])
                        aob_rel = P.emit("sync", lambda e, h0=h: e.dma_start(
                            out=aostage[h0 * 128:(h0 + 1) * 128, :], in_=aob[:]), "sta", 16)
                P.barrier()

            with contextlib.ExitStack() as prs:
                psb = lambda n, s, d: prs.enter_context(nc.sbuf_tensor(_nm(n), s, d))
                aor = psb("pr_aor", [128, NKC * T], BF16)
                wpb = [psb("pr_wp0", [128, 2048], BF16), psb("pr_wp1", [128, 2048], BF16)]
                x2c = [psb("pr_x2c0", [128, T], F32), psb("pr_x2c1", [128, T], F32)]
                x1c = [psb("pr_x1c0", [128, T], F32), psb("pr_x1c1", [128, T], F32)]
                P.emit("sync", lambda e: e.dma_start(
                    out=aor.ap().rearrange("p (k t) -> p k t", k=NKC),
                    in_=aostage.rearrange("(k p) t -> p k t", p=128)), "qst", 16)
                aor_ld = P.cnt["qst"]
                with contextlib.ExitStack() as pes:
                    ps_pr = [pes.enter_context(nc.psum_tensor(_nm("ps_pr%d" % i), [128, 512], F32))
                             for i in range(2)]
                    P.wait("tensor", "qst", aor_ld)
                    wp_use = [0, 0]
                    pr_rel = {0: 0, 1: 0}
                    x1_ld = [0, 0]
                    x2_rel = [0, 0]
                    x2_st = [0, 0]
                    for d0 in range(16):
                        wpar = d0 % 2
                        sg = "wa%d" % wpar
                        if wp_use[wpar]:
                            P.wait("gpsimd", "t", wp_use[wpar])
                        lw = P.emit("gpsimd", lambda e,
                                    b=wpb[wpar].ap().rearrange("p (k i) -> p k i", k=16),
                                    a=wproj[:, d0 * 128:(d0 + 1) * 128].rearrange(
                                        "(k p) i -> p k i", p=128):
                                    e.dma_start(out=b, in_=a), sg, 16)
                        if x2_rel[wpar]:
                            P.wait("sync", "v", x2_rel[wpar])
                        x1_ld[wpar] = P.emit("sync", lambda e, b=x1c[wpar], d1=d0:
                                             e.dma_start(out=b[:],
                                                         in_=x1d[d1 * 128:(d1 + 1) * 128, :]),
                                             "xl%d" % wpar, 16)
                        P.wait("tensor", sg, lw)
                        if x2_st[wpar]:
                            P.wait("vector", "stp%d" % wpar, x2_st[wpar])
                        for cb in range(4):
                            pp = cb % 2
                            if pr_rel[pp]:
                                P.wait("tensor", "v", pr_rel[pp])
                            for k in range(NKC):
                                P.emit("tensor",
                                       lambda e, k0=k, cb0=cb, b=wpb[wpar], pp0=pp:
                                       e.matmul(ps_pr[pp0][:], b[:, k0 * 128:(k0 + 1) * 128],
                                                aor[:, k0 * T + cb0 * 512: k0 * T + cb0 * 512 + 512],
                                                start=(k0 == 0), stop=(k0 == NKC - 1)),
                                       "t", 1 if k == NKC - 1 else None)
                            mmt = P.cnt["t"]
                            P.wait("vector", "t", mmt)
                            pr_rel[pp] = P.emit("vector",
                                                lambda e, cb0=cb, pp0=pp, d1=d0, wp=wpar:
                                                e.tensor_scalar(x2c[wp][:, cb0 * 512:(cb0 + 1) * 512],
                                                                ps_pr[pp0][:], bproj_s[:, d1:d1 + 1],
                                                                None, ALU.add), "v", 1)
                        wp_use[wpar] = P.cnt["t"]
                        P.wait("vector", "xl%d" % wpar, x1_ld[wpar])
                        vv = P.emit("vector", lambda e, wp=wpar: e.tensor_tensor(
                            x2c[wp][:], x2c[wp][:], x1c[wp][:], ALU.add), "v", 1)
                        x2_rel[wpar] = vv
                        P.wait("sync", "v", vv)
                        x2_st[wpar] = P.emit("sync", lambda e, wp=wpar, d1=d0: e.dma_start(
                            out=x2d[d1 * 128:(d1 + 1) * 128, :], in_=x2c[wp][:]),
                            "stp%d" % wpar, 16)
                P.barrier()

        if upto == "C":
            emit_ffn(lambda k, h: x2d[k * 128:(k + 1) * 128, h * HT:(h + 1) * HT],
                     lambda k, h: outx[k * 128:(k + 1) * 128, h * HT:(h + 1) * HT],
                     "C")

        for nm, (dbg, src) in dump_outs.items():
            P.barrier(["sync"])
            P.emit("sync", lambda e, d=dbg, s0=src: e.dma_start(out=d, in_=s0), "qst", 16)

        for s_ in ("qst", "qgst", "stq0", "stq1", "stv0", "stv1", "sta",
                   "stx0", "stx1", "stp0", "stp1"):
            P.wait("sync", s_)

        block = es.enter_context(nc.Block())

        @block.sync
        def _(e):
            for f_ in P.ops["sync"]:
                f_(e)

        @block.gpsimd
        def _(e):
            for f_ in P.ops["gpsimd"]:
                f_(e)

        @block.tensor
        def _(e):
            for f_ in P.ops["tensor"]:
                f_(e)

        @block.vector
        def _(e):
            for f_ in P.ops["vector"]:
                f_(e)

        @block.scalar
        def _(e):
            for f_ in P.ops["scalar"]:
                f_(e)

    return nc



def _prep_inputs(L0_attn_x, L0_attn_Wqkv, L0_attn_bqkv, L0_attn_Wproj,
                 L0_attn_bproj, L0_ffn_w_gate, L0_ffn_w_up, L0_ffn_w_down):
    x = np.asarray(L0_attn_x, np.float32)
    wqkv_ = np.ascontiguousarray(np.asarray(L0_attn_Wqkv, np.float32))
    wproj_ = np.ascontiguousarray(np.asarray(L0_attn_Wproj, np.float32))
    bqkv_ = np.asarray(L0_attn_bqkv, np.float32)
    bproj_ = np.asarray(L0_attn_bproj, np.float32)
    wgf = np.asarray(L0_ffn_w_gate, np.float32)
    wuf = np.asarray(L0_ffn_w_up, np.float32)
    wdf = np.asarray(L0_ffn_w_down, np.float32)

    wg_p = np.ascontiguousarray(
        wgf.reshape(2048, 16, 128, 4).transpose(0, 1, 3, 2).reshape(2048, 8192))
    wu_p = np.ascontiguousarray(
        wuf.reshape(2048, 16, 128, 4).transpose(0, 1, 3, 2).reshape(2048, 8192))
    wd_p = np.ascontiguousarray(
        wdf.reshape(16, 128, 4, 2048).transpose(0, 2, 1, 3).reshape(8192, 2048))
    bqkv2d = np.ascontiguousarray(bqkv_.reshape(48, 128).T)
    bproj2d = np.ascontiguousarray(bproj_.reshape(16, 128).T)

    in_maps = []
    tok_ids_all = []
    for core in range(8):
        seq, j = core // 4, core % 4
        tok = (np.arange(16)[:, None] * 512 + j * 128 + np.arange(128)[None, :]).reshape(-1)
        tok_ids_all.append((seq, tok))
        x0f = np.ascontiguousarray(x[seq][tok, :].T)
        q = np.arange(128)[:, None]
        w = np.arange(512)[None, :]
        mask = np.where(w <= j * 128 + q, 0.0, NEGBIG).astype(np.float32)
        in_maps.append({
            "x0": x0f, "wqkv": wqkv_, "wproj": wproj_, "wg": wg_p, "wu": wu_p,
            "wd": wd_p, "bqkv": bqkv2d, "bproj": bproj2d, "maskin": mask,
        })
    return in_maps, tok_ids_all


def kernel(L0_attn_x, L0_attn_Wqkv, L0_attn_bqkv, L0_attn_Wproj, L0_attn_bproj,
           L0_ffn_w_gate, L0_ffn_w_up, L0_ffn_w_down):
    in_maps, tok_ids_all = _prep_inputs(
        L0_attn_x, L0_attn_Wqkv, L0_attn_bqkv, L0_attn_Wproj, L0_attn_bproj,
        L0_ffn_w_gate, L0_ffn_w_up, L0_ffn_w_down)
    nc = build_program("C")
    res = run_bass_kernel_spmd(nc, in_maps, core_ids=list(range(8)))
    out = np.zeros((2, 8192, 2048), np.float32)
    for core in range(8):
        seq, tok = tok_ids_all[core]
        out[seq][tok, :] = np.asarray(res.results[core]["outx"], np.float32).T
    return out
